# version 15
# baseline (speedup 1.0000x reference)
import math
import numpy as np
import ml_dtypes
import concourse.bass as bass
import concourse.mybir as mybir
from concourse.bass_utils import run_bass_kernel_spmd

F32 = mybir.dt.float32
BF16 = mybir.dt.bfloat16
U8 = mybir.dt.uint8
AF = mybir.ActivationFunctionType
ALU = mybir.AluOpType
AX = mybir.AxisListType

S = 2048
DM = 1024
C = 8
TT = 512
NTT = S // TT
NH = 16
HD = 64
HS = 1024
NEG = -1.0e30

GLOBAL_TOKENS = [0, 292, 584, 877, 1169, 1462, 1754, 2047]

ENGS = ("T", "V", "S", "G", "QS")
import os as _os
STRICT_SYNC = not bool(_os.environ.get("MK_NO_STRICT_SYNC"))
BLOCK_ATTR = {"T": "tensor", "V": "vector", "S": "scalar", "G": "gpsimd", "QS": "sync"}


class APK:
    __slots__ = ("ap", "keys")

    def __init__(self, ap, keys):
        self.ap = ap
        self.keys = keys

    def r(self, fn):
        return APK(fn(self.ap), self.keys)


NQSLOT = 4


class Op:
    __slots__ = ("fn", "kind", "waits", "mark", "clock", "slot")

    def __init__(self, fn, kind, waits, clock, slot=None):
        self.fn = fn
        self.kind = kind
        self.waits = waits
        self.mark = False
        self.clock = clock
        self.slot = slot


class Em:
    def __init__(self):
        self.streams = {e: [] for e in ENGS}
        self.res = {}
        self.clock = {e: {} for e in ENGS}
        self.self_waited = {e: -1 for e in ENGS}
        self.dma_ctr = {e: 0 for e in ENGS}

    def ckey(self, de, di):
        op = self.streams[de][di]
        return de if op.kind == "c" else f"{de}#{op.slot}"

    def emit(self, eng, fn, R=(), W=(), kind="c", force=False):
        rkeys = [k for x in R for k in (x.keys if isinstance(x, APK) else (x,))]
        wkeys = [k for x in W for k in (x.keys if isinstance(x, APK) else (x,))]
        deps = []
        for k in rkeys:
            st = self.res.get(k)
            if st:
                deps.extend(st[0])
        for k in wkeys:
            st = self.res.get(k)
            if st:
                deps.extend(st[0])
                deps.extend(st[1])
        myclock = dict(self.clock[eng])
        waits = []
        seen = set()
        slot = None
        if kind == "d":
            slot = self.dma_ctr[eng] % NQSLOT
            self.dma_ctr[eng] += 1
        for de, di in deps:
            if (de, di) in seen:
                continue
            seen.add((de, di))
            pkind = self.streams[de][di].kind
            if de == eng and pkind == "c" and kind == "c":
                if eng == "T":
                    continue
                is_raw = any(k in rkeys and (de, di) in self.res.get(k, [[], []])[0] for k in rkeys)
                if not is_raw and not force and not STRICT_SYNC:
                    continue
                if di <= self.self_waited[eng]:
                    continue
                waits.append((de, di))
                self.self_waited[eng] = di
                self.streams[de][di].mark = True
                continue
            ck = self.ckey(de, di)
            if myclock.get(ck, -1) >= di:
                continue
            waits.append((de, di))
        for de, di in waits:
            oc = self.streams[de][di].clock
            for k2, v2 in oc.items():
                if myclock.get(k2, -1) < v2:
                    myclock[k2] = v2
            ck = self.ckey(de, di)
            if myclock.get(ck, -1) < di:
                myclock[ck] = di
            self.streams[de][di].mark = True
        idx = len(self.streams[eng])
        myclock[eng if kind == "c" else f"{eng}#{slot}"] = idx
        op = Op(fn, kind, waits, myclock, slot)
        self.streams[eng].append(op)
        self.clock[eng] = myclock
        for k in rkeys:
            self.res.setdefault(k, [[], []])[1].append((eng, idx))
        for k in wkeys:
            st = self.res.setdefault(k, [[], []])
            if kind == "c":
                st[0] = [w for w in st[0]
                         if not (w[0] == eng and self.streams[w[0]][w[1]].kind == "c")]
            st[0].append((eng, idx))
            st[1] = []
        return (eng, idx)


class TV:

    __slots__ = ("ap0", "space", "base", "esz", "pg", "cols", "parts")

    def __init__(self, ap0, space, base_byte, elem_bytes, cols, parts, pg=2048):
        self.ap0 = ap0
        self.space = space
        self.base = base_byte
        self.esz = elem_bytes
        self.pg = pg
        self.cols = cols
        self.parts = parts

    def __getitem__(self, idx):
        ps, cs = idx
        if isinstance(cs, slice):
            c0 = cs.start or 0
            c1 = cs.stop if cs.stop is not None else self.cols
        else:
            c0, c1 = cs, cs + 1
        b0 = self.base + c0 * self.esz
        b1 = self.base + c1 * self.esz
        keys = [f"{self.space}:{p}" for p in range(b0 // self.pg, (b1 - 1) // self.pg + 1)]
        return APK(self.ap0[ps, cs], keys)


def _silu(x):
    return x / (1 + np.exp(-x))


class Prog:

    def __init__(self, phases, name="mk"):
        self.phases = phases
        self.em = Em()
        nc = bass.Bass("TRN2", target_bir_lowering=False, debug=False)
        self.nc = nc
        self._ctxs = []
        self.in_names = set()

        self.d_x = nc.dram_tensor("x", [C, 128, S], F32, kind="ExternalInput").ap()
        self.d_y = nc.dram_tensor("y", [C, 128, S], F32, kind="ExternalOutput").ap()
        self.in_names.add("x")
        self.dram = {}

        def sbt(nm, shape, dt):
            t = self._enter(nc.sbuf_tensor(nm, shape, dt))
            return t

        self.t_xr = sbt("xr", [128, C * S], F32)
        self.xr = TV(self.t_xr[:, :], "xr", 0, 4, C * S, 128)
        self.t_xh = sbt("xh", [128, C * S], BF16)
        self.xh = TV(self.t_xh[:, :], "xh", 0, 2, C * S, 128)

        BIGB = 90624
        self.t_big = sbt("big", [128, BIGB], U8)
        self.bigb = BIGB

        self.t_rowA = sbt("rowA", [1, 1408], BF16)
        self.rowA = TV(self.t_rowA[:, :], "rowA", 0, 2, 1408, 1)
        self.t_rowB = sbt("rowB", [1, 1408], BF16)
        self.rowB = TV(self.t_rowB[:, :], "rowB", 0, 2, 1408, 1)
        self.t_rowC = sbt("rowC", [1, 1024], BF16)
        self.rowC = TV(self.t_rowC[:, :], "rowC", 0, 2, 1024, 1)
        self.t_ones_r = sbt("ones_r", [1, 512], BF16)
        self.ones_r = TV(self.t_ones_r[:, :], "ones_r", 0, 2, 512, 1)
        self.t_ones_c = sbt("ones_c", [128, 1], BF16)
        self.ones_c = TV(self.t_ones_c[:, :], "ones_c", 0, 2, 1, 128)
        self.t_ones_f = sbt("ones_f", [1, 128], F32)
        self.ones_f = TV(self.t_ones_f[:, :], "ones_f", 0, 4, 128, 1)
        self.t_mrow = sbt("mrow", [1, 512], F32)
        self.mrow = TV(self.t_mrow[:, :], "mrow", 0, 4, 512, 1)
        self.t_zrow = sbt("zrow", [1, 512], F32)
        self.zrow = TV(self.t_zrow[:, :], "zrow", 0, 4, 512, 1)
        self.t_tmp1 = sbt("tmp1", [128, 512], BF16)
        self.tmp1 = TV(self.t_tmp1[:, :], "tmp1", 0, 2, 512, 128)
        self.t_tmp2 = sbt("tmp2", [128, 512], BF16)
        self.tmp2 = TV(self.t_tmp2[:, :], "tmp2", 0, 2, 512, 128)
        self.t_tmpf = sbt("tmpf", [128, 512], F32)
        self.tmpf = TV(self.t_tmpf[:, :], "tmpf", 0, 4, 512, 128)
        self.t_tmpg = sbt("tmpg", [128, 512], F32)
        self.tmpg = TV(self.t_tmpg[:, :], "tmpg", 0, 4, 512, 128)
        self.t_fstat = sbt("fstat", [128, 96], F32)
        self.fstat = TV(self.t_fstat[:, :], "fstat", 0, 4, 96, 128)
        self.t_fscr = sbt("fscr", [128, 128], F32)
        self.fscr = TV(self.t_fscr[:, :], "fscr", 0, 4, 128, 128)
        self.t_probs = sbt("probs", [128, 128], F32)
        self.probs = TV(self.t_probs[:, :], "probs", 0, 4, 128, 128)
        self.t_eps = sbt("epsc", [1, 8], F32)
        self.epsc = TV(self.t_eps[:, :], "epsc", 0, 4, 8, 1)
        self.t_bcA = sbt("bcA", [128, 12], F32)
        self.bcA = TV(self.t_bcA[:, :], "bcA", 0, 4, 12, 128)
        self.t_bcB = sbt("bcB", [128, 12], F32)
        self.bcB = TV(self.t_bcB[:, :], "bcB", 0, 4, 12, 128)
        self.t_bcC = sbt("bcC", [128, 12], F32)
        self.bcC = TV(self.t_bcC[:, :], "bcC", 0, 4, 12, 128)
        self.t_idbf = sbt("idbf", [128, 128], BF16)
        self.idbf = TV(self.t_idbf[:, :], "idbf", 0, 2, 128, 128)
        self.t_tri = sbt("tri", [128, 128], BF16)
        self.tri = TV(self.t_tri[:, :], "tri", 0, 2, 128, 128)
        self.t_quad = sbt("quad", [128, 1024], BF16)
        self.quad = TV(self.t_quad[:, :], "quad", 0, 2, 1024, 128)

        self.t_ps = self._enter(nc.psum_tensor("ps", [128, 4096], F32))
        self.ps = TV(self.t_ps[:, :], "ps", 0, 4, 4096, 128)
        self.psb = TV(self.t_ps[:, :].bitcast(BF16), "ps", 0, 2, 8192, 128)

        self.NQ = 4
        self.sems = {e: self._enter(nc.semaphore("sem" + e)) for e in ENGS}
        self.qsems = {e: [self._enter(nc.semaphore(f"q{e}{i}")) for i in range(self.NQ)]
                      for e in ("G", "QS")}

        for nm, shape in [("c_id", [128, 128]), ("c_tri", [128, 128]), ("c_quad", [128, 1024]),
                          ("c_ones_r", [1, 512]), ("c_ones_c", [128, 1])]:
            self.dram[nm] = nc.dram_tensor(nm, shape, BF16, kind="ExternalInput").ap()
            self.in_names.add(nm)
        self.dram["c_ones_f"] = nc.dram_tensor("c_ones_f", [1, 128], F32, kind="ExternalInput").ap()
        self.in_names.add("c_ones_f")
        self.dram["c_eps"] = nc.dram_tensor("c_eps", [1, 8], F32, kind="ExternalInput").ap()
        self.in_names.add("c_eps")

    def _enter(self, cm):
        v = cm.__enter__()
        self._ctxs.append(cm)
        return v

    def din(self, name, shape):
        if name not in self.dram:
            self.dram[name] = self.nc.dram_tensor(name, shape, F32, kind="ExternalInput").ap()
            self.in_names.add(name)
        return self.dram[name]

    def carve(self, byte_off, parts, cols, dt):
        esz = 2 if dt == BF16 else 4
        nbytes = cols * esz
        assert byte_off % 4 == 0 and byte_off + nbytes <= self.bigb, (byte_off, nbytes)
        ap = self.t_big[0:parts, byte_off:byte_off + nbytes].bitcast(dt)
        return TV(ap, "big", byte_off, esz, cols, parts)

    def bank(self, b, dt=F32):
        if dt == F32:
            return TV(self.t_ps[:, b * 512:(b + 1) * 512], "ps", b * 2048, 4, 512, 128)
        return TV(self.t_ps[:, b * 512:(b + 1) * 512].bitcast(BF16), "ps", b * 2048, 2, 1024, 128)

    def banks(self, b0, nb, dt=F32):
        if dt == F32:
            return TV(self.t_ps[:, b0 * 512:(b0 + nb) * 512], "ps", b0 * 2048, 4, nb * 512, 128)
        return TV(self.t_ps[:, b0 * 512:(b0 + nb) * 512].bitcast(BF16), "ps", b0 * 2048, 2, nb * 1024, 128)

    def E(self, eng, fn, R=(), W=(), kind="c", force=False):
        return self.em.emit(eng, fn, R, W, kind, force)

    def dma(self, q, dst, src, R=(), W=()):
        s_ap = src.ap if isinstance(src, APK) else src
        d_ap = dst.ap if isinstance(dst, APK) else dst
        rr = [src] if isinstance(src, APK) else list(R)
        ww = [dst] if isinstance(dst, APK) else list(W)
        self.em.emit(q, lambda e: e.dma_start(out=d_ap, in_=s_ap), rr + list(R), ww + list(W), kind="d")

    def fm(self, tv, c, t0, t1, stride=S):
        return tv[:, c * stride + t0: c * stride + t1]

    def finalize(self):
        em = self.em
        NQ = self.NQ
        assert NQ == NQSLOT
        self.vals = {}
        self.dslot = {}
        for e in ENGS:
            vs, cnt = [], 0
            slots = [0] * NQ
            dinfo = []
            for op in em.streams[e]:
                if op.kind == "d":
                    sl = op.slot
                    slots[sl] += 1
                    dinfo.append((sl, slots[sl]))
                    vs.append(cnt)
                else:
                    if op.mark:
                        cnt += 1
                    vs.append(cnt)
                    dinfo.append(None)
            self.vals[e] = vs
            self.dslot[e] = dinfo
        nc = self.nc
        sems, qsems, vals, dslot, streams = self.sems, self.qsems, self.vals, self.dslot, em.streams

        def wait_args(de, di):
            if streams[de][di].kind == "d":
                sl, c = dslot[de][di]
                return qsems[de][sl], 16 * c
            return sems[de], vals[de][di]

        with nc.Block() as block:
            for eng in ENGS:
                if not streams[eng]:
                    continue

                def mk(eng):
                    def body(e):
                        sem_c = sems[eng]
                        for i, op in enumerate(streams[eng]):
                            for (de, di) in op.waits:
                                sem, v = wait_args(de, di)
                                e.wait_ge(sem, v)
                            if op.fn is None:
                                continue
                            if op.kind == "d":
                                sl, c = dslot[eng][i]
                                if c > 1:
                                    e.wait_ge(qsems[eng][sl], 16 * (c - 1))
                                inst = op.fn(e)
                                inst.then_inc(qsems[eng][sl], 16)
                            else:
                                inst = op.fn(e)
                                if op.mark:
                                    inst.then_inc(sem_c, 1)
                    return body

                getattr(block, BLOCK_ATTR[eng])(mk(eng))
        return nc



def emit_bcol(P, row, o, dst, scale=None):
    E = P.E
    rs = row[0:1, o * 128:(o + 1) * 128]
    pt = P.bank(7, BF16)[0:128, (o % 4) * 2:(o % 4) * 2 + 1]
    E("T", lambda e, ot=pt.ap, i=rs.ap, idm=P.idbf[0:1, 0:1].ap:
      e.transpose(out=ot, in_=i, identity=idm), [rs, P.idbf[0:1, 0:1]], [pt])
    d_ = dst[:, o:o + 1]
    if scale is None:
        E("V", lambda e, ot=d_.ap, i=pt.ap: e.tensor_copy(out=ot, in_=i), [pt], [d_])
    else:
        E("V", lambda e, ot=d_.ap, i=pt.ap, sc=scale:
          e.tensor_single_scalar(out=ot, in_=i, scalar=sc, op=ALU.mult), [pt], [d_])


def emit_load_x(P):
    for c in range(C):
        P.dma("QS", P.xr[:, c * S:(c + 1) * S], APK(P.d_x[c], []))


def emit_store_x(P):
    for c in range(C):
        src = P.xr[:, c * S:(c + 1) * S]
        P.em.emit("QS", lambda e, s=src.ap, d=P.d_y[c]: e.dma_start(out=d, in_=s),
                  [src], ["dram:y"], kind="d")
    P.em.emit("QS", None, R=["dram:y"], W=())


def emit_consts(P):
    P.dma("QS", APK(P.idbf.ap0, ["idbf:0"]), APK(P.dram["c_id"], []))
    P.dma("QS", APK(P.tri.ap0, ["tri:0"]), APK(P.dram["c_tri"], []))
    P.dma("QS", APK(P.quad.ap0, ["quad:0"]), APK(P.dram["c_quad"], []))
    P.dma("QS", APK(P.ones_r.ap0, ["ones_r:0"]), APK(P.dram["c_ones_r"], []))
    P.dma("QS", APK(P.ones_c.ap0, ["ones_c:0"]), APK(P.dram["c_ones_c"], []))
    P.dma("QS", APK(P.ones_f.ap0, ["ones_f:0"]), APK(P.dram["c_ones_f"], []))
    P.dma("QS", APK(P.epsc.ap0, ["epsc:0"]), APK(P.dram["c_eps"], []))


def emit_ln(P):
    em, E = P.em, P.E
    for tt in range(NTT):
        t0, t1 = tt * TT, (tt + 1) * TT
        for c in range(C):
            xrt = P.fm(P.xr, c, t0, t1)
            a = P.tmp1[:, :]
            b = P.tmp2[:, :]
            E("S", lambda e, o=a.ap, i=xrt.ap: e.copy(out=o, in_=i), [xrt], [a])
            E("S", lambda e, o=b.ap, i=a.ap: e.activation(out=o, in_=i, func=AF.Square), [a], [b])
            psum_s = P.bank(2)[0:1, 0:TT]
            psum_q = P.bank(3)[0:1, 0:TT]
            st = (c == 0)
            sp = (c == C - 1)
            E("T", lambda e, o=psum_s.ap, l=P.ones_c[:, 0:1].ap, r=a.ap, st=st, sp=sp:
              e.matmul(out=o, lhsT=l, rhs=r, start=st, stop=sp), [a, P.ones_c[:, 0:1]], [psum_s])
            E("T", lambda e, o=psum_q.ap, l=P.ones_c[:, 0:1].ap, r=b.ap, st=st, sp=sp:
              e.matmul(out=o, lhsT=l, rhs=r, start=st, stop=sp), [b, P.ones_c[:, 0:1]], [psum_q])
        psum_s = P.bank(2)[0:1, 0:TT]
        psum_q = P.bank(3)[0:1, 0:TT]
        mr = P.mrow[0:1, 0:TT]
        zr = P.zrow[0:1, 0:TT]
        E("V", lambda e, o=mr.ap, i=psum_s.ap: e.tensor_single_scalar(out=o, in_=i, scalar=1.0 / DM,
                                                                     op=ALU.mult), [psum_s], [mr])
        E("V", lambda e, o=zr.ap, i=mr.ap: e.tensor_tensor(out=o, in0=i, in1=i, op=ALU.mult), [mr], [zr])
        E("V", lambda e, o=zr.ap, i=psum_q.ap, s=zr.ap:
          e.scalar_tensor_tensor(out=o, in0=i, scalar=1.0 / DM, in1=s, op0=ALU.mult, op1=ALU.subtract),
          [psum_q, zr], [zr])
        E("S", lambda e, o=zr.ap, i=zr.ap, b=P.epsc[0:1, 0:1].ap:
          e.activation(out=o, in_=i, func=AF.Ln, bias=b), [zr, P.epsc[0:1, 0:1]], [zr])
        E("S", lambda e, o=zr.ap, i=zr.ap: e.activation(out=o, in_=i, func=AF.Exp, scale=-0.5), [zr], [zr])
        mrb = P.mrow[0:1, TT:TT + TT] if False else None
        mb16 = P.tmp1[0:1, 0:TT]
        zb16 = P.tmp2[0:1, 0:TT]
        E("S", lambda e, o=mb16.ap, i=mr.ap: e.copy(out=o, in_=i), [mr], [mb16])
        E("S", lambda e, o=zb16.ap, i=zr.ap: e.copy(out=o, in_=i), [zr], [zb16])
        mbc = P.bank(6)[:, 0:TT]
        zbc = P.bank(7)[:, 0:TT]
        E("T", lambda e, o=mbc.ap, l=P.ones_r[0:1, 0:128].ap, r=mb16.ap:
          e.matmul(out=o, lhsT=l, rhs=r, start=True, stop=True), [mb16, P.ones_r[0:1, 0:128]], [mbc])
        E("T", lambda e, o=zbc.ap, l=P.ones_r[0:1, 0:128].ap, r=zb16.ap:
          e.matmul(out=o, lhsT=l, rhs=r, start=True, stop=True), [zb16, P.ones_r[0:1, 0:128]], [zbc])
        for c in range(C):
            xrt = P.fm(P.xr, c, t0, t1)
            xht = P.fm(P.xh, c, t0, t1)
            tf = P.tmpf[:, :]
            E("V", lambda e, o=tf.ap, i=xrt.ap, m=mbc.ap: e.tensor_tensor(out=o, in0=i, in1=m, op=ALU.subtract),
              [xrt, mbc], [tf])
            E("V", lambda e, o=xht.ap, i=tf.ap, z=zbc.ap: e.tensor_tensor(out=o, in0=i, in1=z, op=ALU.mult),
              [tf, zbc], [xht])



def emit_moe_ffn(P, layer, experts=None):
    em, E = P.em, P.E
    nE = 8
    if layer == 0:
        dh, dhs = 512, 1024
        OC, OCS = dh // 128, dhs // 128
        has_bias = False
        silu_prod = False
        w_names = ("L0_ffn_experts_gate", "L0_ffn_experts_up", "L0_ffn_experts_down")
        s_names = ("L0_ffn_shared_gate", "L0_ffn_shared_up", "L0_ffn_shared_down")
        gate_name, gate_is_he = "L0_ffn_router", True
        b_names = s_bnames = None
        sel_biased = False
    else:
        dh = dhs = 1408
        OC = OCS = 11
        has_bias = True
        silu_prod = True
        w_names = ("L1_ffn_w1", "L1_ffn_w3", "L1_ffn_w2")
        s_names = ("L1_ffn_sw1", "L1_ffn_sw3", "L1_ffn_sw2")
        b_names = ("L1_ffn_b1", "L1_ffn_b3", "L1_ffn_b2")
        s_bnames = ("L1_ffn_sb1", "L1_ffn_sb3", "L1_ffn_sb2")
        gate_name, gate_is_he = "L1_ffn_gate_w", False
        sel_biased = True

    dmax = max(dh, dhs)
    w1v = P.carve(0, 128, C * dmax, BF16)
    w3v = P.carve(22528, 128, C * dmax, BF16)
    w2v = P.carve(45056, 128, (dmax // 128) * 1024, BF16)
    hv = P.carve(67584, 128, (dmax // 128) * TT, BF16)
    cbc = P.carve(79360, 128, 512, F32)
    ctile = P.carve(81408, 8, 2048, BF16)
    coefrow = P.carve(85504, 1, 2048, BF16)
    gwT = P.carve(89600, 128, 64, BF16)
    gbc = P.carve(89728, 128, 8, F32)
    coef_tm = P.carve(89760, 128, 128, BF16)
    stg = P.carve(81408, 8, 1024, BF16)

    if gate_is_he:
        src = P.din(gate_name, [DM, nE])
        for c in range(C):
            P.dma("G", gwT[:, c * 8:(c + 1) * 8], APK(src[c * 128:(c + 1) * 128, :], []))
    else:
        src = P.din(gate_name, [nE, DM])
        P.dma("G", stg[0:8, 0:1024], APK(src, []))
        for c in range(C):
            pt = P.bank(5, BF16)[0:128, 0:8]
            inp = stg[0:8, c * 128:(c + 1) * 128]
            E("T", lambda e, o=pt.ap, i=inp.ap, idm=P.idbf[0:8, 0:8].ap:
              e.transpose(out=o, in_=i, identity=idm), [inp, P.idbf[0:8, 0:8]], [pt])
            dstc = gwT[:, c * 8:(c + 1) * 8]
            E("V", lambda e, o=dstc.ap, i=pt.ap: e.tensor_copy(out=o, in_=i), [pt], [dstc])
        gb = P.din("L1_ffn_gate_b", [nE])
        P.dma("QS", APK(P.mrow.ap0[0:1, 0:8], ["mrow:0"]), APK(gb.unsqueeze(0), []))
        pd = P.bank(3)[:, 0:8]
        E("T", lambda e, o=pd.ap, l=P.ones_f[0:1, :].ap, r=P.mrow[0:1, 0:8].ap:
          e.matmul(out=o, lhsT=l, rhs=r, start=True, stop=True), ["mrow:0", P.ones_f[0:1, :]], [pd])
        E("V", lambda e, o=gbc[:, 0:8].ap, i=pd.ap: e.tensor_copy(out=o, in_=i), [pd], [gbc[:, 0:8]])

    pa = P.bank(0)
    for tc in range(16):
        out = pa[:, tc * 8:(tc + 1) * 8]
        for c in range(C):
            xt = P.fm(P.xh, c, tc * 128, (tc + 1) * 128)
            g = gwT[:, c * 8:(c + 1) * 8]
            E("T", lambda e, o=out.ap, l=xt.ap, r=g.ap, st=(c == 0), sp=(c == C - 1):
              e.matmul(out=o, lhsT=l, rhs=r, start=st, stop=sp), [xt, g], [out])
    la = pa[:, 0:128]
    lap3 = la.r(lambda ap: ap.rearrange("p (a b) -> p a b", b=8))
    mx = P.fstat[:, 0:16]
    E("V", lambda e, o=mx.ap, i=lap3.ap: e.tensor_reduce(out=o, in_=i, axis=AX.X, op=ALU.max), [la], [mx])
    d1 = P.carve(67584, 128, 128, F32)[:, 0:128]
    d13 = d1.r(lambda ap: ap.rearrange("p (a b) -> p a b", b=8))
    d2 = P.carve(68608, 128, 128, F32)[:, 0:128]
    d23 = d2.r(lambda ap: ap.rearrange("p (a b) -> p a b", b=8))
    E("V", lambda e, o=d13.ap, i=lap3.ap, m=mx.ap.unsqueeze(-1).broadcast_to((128, 16, 8)):
      e.tensor_tensor(out=o, in0=i, in1=m, op=ALU.subtract), [la, mx], [d1])
    E("S", lambda e, o=d2.ap, i=d1.ap: e.activation(out=o, in_=i, func=AF.Exp), [d1], [d2])
    sm = P.fstat[:, 16:32]
    E("V", lambda e, o=sm.ap, i=d23.ap: e.tensor_reduce(out=o, in_=i, axis=AX.X, op=ALU.add), [d2], [sm])
    sr = P.fstat[:, 64:80]
    E("V", lambda e, o=sr.ap, i=sm.ap: e.reciprocal(out=o, in_=i), [sm], [sr], force=True)
    pr = P.probs[:, 0:128]
    pr3 = pr.r(lambda ap: ap.rearrange("p (a b) -> p a b", b=8))
    E("V", lambda e, o=pr3.ap, i=d23.ap, m=sr.ap.unsqueeze(-1).broadcast_to((128, 16, 8)):
      e.tensor_tensor(out=o, in0=i, in1=m, op=ALU.mult), [d2, sr], [pr], force=True)
    sel = P.fscr[:, 0:128]
    sel3 = sel.r(lambda ap: ap.rearrange("p (a b) -> p a b", b=8))
    if sel_biased:
        E("V", lambda e, o=sel3.ap, i=pr3.ap, m=gbc[:, 0:8].ap.unsqueeze(1).broadcast_to((128, 16, 8)):
          e.tensor_tensor(out=o, in0=i, in1=m, op=ALU.add), [pr, gbc[:, 0:8]], [sel])
        base = sel
    else:
        E("V", lambda e, o=sel.ap, i=pr.ap: e.tensor_copy(out=o, in_=i), [pr], [sel])
        base = sel
    m1 = P.fstat[:, 32:48]
    E("V", lambda e, o=m1.ap, i=sel3.ap: e.tensor_reduce(out=o, in_=i, axis=AX.X, op=ALU.max), [sel], [m1])
    eqm = P.tmpf[:, 0:128]
    eqm3 = eqm.r(lambda ap: ap.rearrange("p (a b) -> p a b", b=8))
    E("V", lambda e, o=eqm3.ap, i=sel3.ap, m=m1.ap.unsqueeze(-1).broadcast_to((128, 16, 8)):
      e.tensor_tensor(out=o, in0=i, in1=m, op=ALU.is_ge), [sel, m1], [eqm])
    E("V", lambda e, o=eqm3.ap, i=eqm3.ap, s=sel3.ap:
      e.scalar_tensor_tensor(out=o, in0=i, scalar=NEG, in1=s, op0=ALU.mult, op1=ALU.add), [eqm, sel], [eqm])
    m2 = P.fstat[:, 48:64]
    E("V", lambda e, o=m2.ap, i=eqm3.ap: e.tensor_reduce(out=o, in_=i, axis=AX.X, op=ALU.max), [eqm], [m2])
    E("V", lambda e, o=eqm3.ap, i=sel3.ap, m=m2.ap.unsqueeze(-1).broadcast_to((128, 16, 8)):
      e.tensor_tensor(out=o, in0=i, in1=m, op=ALU.is_ge), [sel, m2], [eqm])
    ctm = coef_tm[:, 0:128]
    ctm3 = ctm.r(lambda ap: ap.rearrange("p (a b) -> p a b", b=8))
    E("V", lambda e, o=ctm3.ap, i=pr3.ap, m=eqm3.ap: e.tensor_tensor(out=o, in0=i, in1=m, op=ALU.mult),
      [pr, eqm], [ctm])
    for tc in range(16):
        pt = P.bank(5, BF16)[0:8, 0:128]
        inp = coef_tm[:, tc * 8:(tc + 1) * 8]
        E("T", lambda e, o=pt.ap, i=inp.ap, idm=P.idbf[:, :].ap: e.transpose(out=o, in_=i, identity=idm),
          [inp, P.idbf[:, :]], [pt])
        dstc = ctile[0:8, tc * 128:(tc + 1) * 128]
        E("V", lambda e, o=dstc.ap, i=pt.ap: e.tensor_copy(out=o, in_=i), [pt], [dstc])

    for e_i in (range(-1, nE) if experts is None else experts):
        routed = e_i >= 0
        oc = OC if routed else OCS
        dloc = dh if routed else dhs
        dkc = dloc // 128
        if routed:
            if has_bias:
                P.dma("G", P.rowA[0:1, 0:dloc], APK(P.din(b_names[0], [nE, dloc])[e_i:e_i + 1, :], []))
                P.dma("G", P.rowB[0:1, 0:dloc], APK(P.din(b_names[1], [nE, dloc])[e_i:e_i + 1, :], []))
                P.dma("G", P.rowC[0:1, 0:1024], APK(P.din(b_names[2], [nE, 1024])[e_i:e_i + 1, :], []))
            for wv, nm in ((w1v, w_names[0]), (w3v, w_names[1])):
                src = P.din(nm, [nE, DM, dloc])
                for c in range(C):
                    P.dma("G", wv[:, c * dloc:(c + 1) * dloc],
                          APK(src[e_i, c * 128:(c + 1) * 128, :], []))
            src = P.din(w_names[2], [nE, dloc, DM])
            for c in range(dkc):
                P.dma("G", w2v[:, c * 1024:(c + 1) * 1024],
                      APK(src[e_i, c * 128:(c + 1) * 128, :], []))
            P.dma("QS", coefrow[0:1, 0:2048], ctile[e_i:e_i + 1, 0:2048])
        else:
            if s_bnames:
                P.dma("G", P.rowA[0:1, 0:dloc], APK(P.din(s_bnames[0], [dloc]).unsqueeze(0), []))
                P.dma("G", P.rowB[0:1, 0:dloc], APK(P.din(s_bnames[1], [dloc]).unsqueeze(0), []))
                P.dma("G", P.rowC[0:1, 0:1024], APK(P.din(s_bnames[2], [DM]).unsqueeze(0), []))
            for wv, nm in ((w1v, s_names[0]), (w3v, s_names[1])):
                src = P.din(nm, [DM, dloc])
                for c in range(C):
                    P.dma("G", wv[:, c * dloc:(c + 1) * dloc],
                          APK(src[c * 128:(c + 1) * 128, :], []))
            src = P.din(s_names[2], [dloc, DM])
            for c in range(dkc):
                P.dma("G", w2v[:, c * 1024:(c + 1) * 1024],
                      APK(src[c * 128:(c + 1) * 128, :], []))

        use_bias = has_bias

        for tt in range(NTT):
            t0, t1 = tt * TT, (tt + 1) * TT
            if routed:
                pd = P.bank(6)[:, 0:TT]
                cr = coefrow[0:1, t0:t1]
                E("T", lambda e, o=pd.ap, l=P.ones_r[0:1, 0:128].ap, r=cr.ap:
                  e.matmul(out=o, lhsT=l, rhs=r, start=True, stop=True), [cr, P.ones_r[0:1, 0:128]], [pd])
                E("V", lambda e, o=cbc[:, 0:TT].ap, i=pd.ap:
                  e.tensor_single_scalar(out=o, in_=i, scalar=0.5, op=ALU.mult), [pd], [cbc[:, 0:TT]])
            for o_i in range(oc):
                pA = P.bank((o_i % 3) * 2)[:, 0:TT]
                pB = P.bank((o_i % 3) * 2 + 1)[:, 0:TT]
                if use_bias and tt == 0:
                    emit_bcol(P, P.rowA, o_i, P.bcA)
                    emit_bcol(P, P.rowB, o_i, P.bcB)
                for (pX, wv_) in ((pA, w1v), (pB, w3v)):
                    for c in range(C):
                        lt = wv_[:, c * dloc + o_i * 128: c * dloc + (o_i + 1) * 128]
                        rt = P.fm(P.xh, c, t0, t1)
                        E("T", lambda e, o=pX.ap, l=lt.ap, r=rt.ap, st=(c == 0), sp=(c == C - 1):
                          e.matmul(out=o, lhsT=l, rhs=r, start=st, stop=sp), [lt, rt], [pX])
                ht = hv[:, o_i * TT:(o_i + 1) * TT]
                if silu_prod:
                    tf = P.tmpf[:, :]
                    ba = P.bcA[:, o_i:o_i + 1]
                    bb = P.bcB[:, o_i:o_i + 1]
                    E("S", lambda e, o=tf.ap, i=pA.ap, b=ba.ap:
                      e.activation(out=o, in_=i, func=AF.Identity, bias=b), [pA, ba], [tf])
                    tg = P.tmpg[:, :]
                    E("V", lambda e, o=tg.ap, i=pB.ap, b=bb.ap, j=tf.ap:
                      e.scalar_tensor_tensor(out=o, in0=i, scalar=b, in1=j, op0=ALU.add, op1=ALU.mult),
                      [pB, bb, tf], [tg])
                    E("S", lambda e, o=ht.ap, i=tg.ap: e.activation(out=o, in_=i, func=AF.Silu), [tg], [ht])
                else:
                    t1b = P.tmp1[:, :]
                    E("S", lambda e, o=t1b.ap, i=pA.ap: e.activation(out=o, in_=i, func=AF.Silu), [pA], [t1b])
                    E("V", lambda e, o=ht.ap, i=pB.ap, j=t1b.ap: e.tensor_tensor(out=o, in0=i, in1=j, op=ALU.mult),
                      [pB, t1b], [ht])
            for d_i in range(C):
                pC = P.bank(6 + d_i % 2)[:, 0:TT]
                if use_bias and tt == 0:
                    emit_bcol(P, P.rowC, d_i, P.bcC)
                for k in range(dkc):
                    lt = w2v[:, k * 1024 + d_i * 128: k * 1024 + (d_i + 1) * 128]
                    rt = hv[:, k * TT:(k + 1) * TT]
                    E("T", lambda e, o=pC.ap, l=lt.ap, r=rt.ap, st=(k == 0), sp=(k == dkc - 1):
                      e.matmul(out=o, lhsT=l, rhs=r, start=st, stop=sp), [lt, rt], [pC])
                xrt = P.fm(P.xr, d_i, t0, t1)
                bc_ = P.bcC[:, d_i:d_i + 1]
                if routed:
                    tf = P.tmpf[:, :]
                    if use_bias:
                        E("V", lambda e, o=tf.ap, i=pC.ap, b=bc_.ap, j=cbc[:, 0:TT].ap:
                          e.scalar_tensor_tensor(out=o, in0=i, scalar=b, in1=j, op0=ALU.add, op1=ALU.mult),
                          [pC, bc_, cbc[:, 0:TT]], [tf])
                    else:
                        E("V", lambda e, o=tf.ap, i=pC.ap, j=cbc[:, 0:TT].ap:
                          e.tensor_tensor(out=o, in0=i, in1=j, op=ALU.mult), [pC, cbc[:, 0:TT]], [tf])
                    E("G", lambda e, o=xrt.ap, i=tf.ap, x=xrt.ap:
                      e.tensor_tensor(out=o, in0=i, in1=x, op=ALU.add), [tf, xrt], [xrt])
                elif use_bias:
                    tf = P.tmpf[:, :]
                    E("V", lambda e, o=tf.ap, i=pC.ap, b=bc_.ap:
                      e.tensor_scalar(out=o, in0=i, scalar1=b, scalar2=0.5, op0=ALU.add, op1=ALU.mult),
                      [pC, bc_], [tf])
                    E("G", lambda e, o=xrt.ap, i=tf.ap, x=xrt.ap:
                      e.tensor_tensor(out=o, in0=i, in1=x, op=ALU.add), [tf, xrt], [xrt])
                else:
                    E("V", lambda e, o=xrt.ap, i=pC.ap, x=xrt.ap:
                      e.scalar_tensor_tensor(out=o, in0=i, scalar=0.5, in1=x, op0=ALU.mult, op1=ALU.add),
                      [pC, xrt], [xrt])



def _make_consts():
    idm = np.eye(128, dtype=ml_dtypes.bfloat16)
    tri = np.triu(np.ones((128, 128), dtype=np.float32)).astype(ml_dtypes.bfloat16)
    quad = np.full((128, 8 * 128), NEG, dtype=np.float32)
    q = np.zeros((128, 128), dtype=np.float32)
    q[0:64, 0:64] = 1
    q[64:128, 64:128] = 1
    for j in range(8):
        quad[:, j * 128:(j + 1) * 128] = np.where(q > 0, 0.0, NEG)
    return {
        "c_id": idm,
        "c_tri": tri,
        "c_quad": quad.astype(ml_dtypes.bfloat16),
        "c_ones_r": np.ones((1, 512), dtype=ml_dtypes.bfloat16),
        "c_ones_c": np.ones((128, 1), dtype=ml_dtypes.bfloat16),
        "c_ones_f": np.ones((1, 128), dtype=np.float32),
        "c_eps": np.array([[1e-5, 1e-6, 0, 0, 0, 0, 0, 0]], dtype=np.float32),
    }


PHASE_EMIT = {}


def _register_phases():
    PHASE_EMIT.update({
        "ffn0": lambda P: (emit_ln(P), emit_moe_ffn(P, 0)),
        "ffn1": lambda P: (emit_ln(P), emit_moe_ffn(P, 1)),
        "attn0": lambda P: (emit_ln(P), emit_attn0(P)),
        "attn1": lambda P: (emit_ln(P), emit_attn1(P)),
    })


def build_program(phases, debug=()):
    P = Prog(phases)
    emit_consts(P)
    emit_load_x(P)
    for ph in phases:
        PHASE_EMIT[ph](P)
    emit_store_x(P)
    for i, (nm, tv, p0, p1, c0, c1) in enumerate(debug):
        dt = BF16 if tv(P).esz == 2 else F32
        dd = P.nc.dram_tensor("dbg_" + nm, [p1 - p0, c1 - c0], dt, kind="ExternalOutput").ap()
        reg = tv(P)[p0:p1, c0:c1]
        P.em.emit("QS", lambda e, d=dd, s=reg.ap: e.dma_start(out=d, in_=s), [reg], ["dram:dbg"], kind="d")
    if debug:
        P.em.emit("QS", None, R=["dram:dbg"], W=())
    nc = P.finalize()
    return P, nc


ALL_PHASES = ["ffn0", "attn0", "ffn0", "ffn1", "attn1", "ffn1"]


def kernel(**inputs):
    _register_phases()
    consts = _make_consts()
    x = np.asarray(inputs["L0_attn_x"], dtype=np.float32)
    n = 8
    base = {k: np.asarray(v, dtype=np.float32) for k, v in inputs.items() if k != "L0_attn_x"}

    xs = [np.ascontiguousarray(x[i].T).reshape(C, 128, S) for i in range(n)]

    import os
    if os.environ.get("MK_UNFUSED"):
        groups = [[ph] for ph in ALL_PHASES]
    else:
        groups = [ALL_PHASES]
    cur = xs
    for g in groups:
        P, nc = build_program(g)
        in_maps = []
        for i in range(n):
            m = {"x": cur[i]}
            for nm in P.in_names:
                if nm == "x":
                    continue
                if nm.startswith("c_"):
                    m[nm] = consts[nm]
                else:
                    m[nm] = base[nm]
            in_maps.append(m)
        res = run_bass_kernel_spmd(nc, in_maps, core_ids=list(range(n)))
        cur = [res.results[i]["y"] for i in range(n)]

    out = np.stack([cur[i].reshape(DM, S).T for i in range(n)]).astype(np.float32)
    return out





def _attn_carves(P):
    c = {}
    c["K"] = P.carve(0, 128, 8 * 1024, BF16)
    c["Q"] = P.carve(16384, 128, 8 * 1024, BF16)
    c["V"] = P.carve(32768, 128, 8 * 1024, BF16)
    c["W"] = P.carve(49152, 128, 8 * 1024, BF16)
    c["pT"] = P.carve(65536, 128, 1024, BF16)
    c["pTg"] = P.carve(67584, 8, 1024, BF16)
    c["ssb"] = P.carve(69632, 128, 1024, BF16)
    c["png"] = P.carve(71680, 128, 128, BF16)
    c["kg"] = P.carve(71936, 128, 64, BF16)
    c["kg2"] = P.carve(84576, 128, 128, BF16)
    c["Qz"] = P.carve(84832, 128, 2048, BF16)
    c["qg"] = P.carve(72064, 128, 64, BF16)
    c["xhg"] = P.carve(72192, 128, 64, BF16)
    c["vg"] = P.carve(72320, 8, 1024, BF16)
    c["gp"] = P.carve(74368, 128, 1024, BF16)
    c["gst"] = P.carve(76416, 128, 40, F32)
    c["og0"] = P.carve(76576, 64, 128, F32)
    c["og1"] = P.carve(77088, 64, 128, F32)
    c["ogf"] = P.carve(77600, 64, 128, BF16)
    c["cb0"] = P.carve(77856, 64, 128, BF16)
    c["cb1"] = P.carve(78112, 64, 128, BF16)
    c["crow"] = P.carve(78368, 1, 256, BF16)
    c["KVs"] = P.carve(78880, 128, 16 * 65, F32)
    c["am"] = P.carve(83040, 128, 512, BF16)
    c["ktm"] = P.carve(84064, 128, 256, BF16)
    c["KVb"] = P.carve(84832, 128, 16 * 65, BF16)
    c["rr"] = P.carve(74368, 1, 512, F32)
    c["rb"] = P.carve(76416, 64, 512, F32)
    return c


def _proj_fm(P, W, dst, brow, t0, t1, scale=None, phi=False, bcol=None, build_cols=False):
    E = P.E
    h0 = (t0 // HS) * HS
    for oc in range(C):
        if build_cols:
            emit_bcol(P, brow, oc, bcol, scale=scale)
        bc_ = bcol[:, oc:oc + 1]
        for tt0 in range(t0, t1, TT):
            pX = P.bank(oc % 4)[:, 0:TT]
            for ch in range(C):
                lt = W[:, ch * 1024 + oc * 128: ch * 1024 + (oc + 1) * 128]
                rt = P.fm(P.xh, ch, tt0, tt0 + TT)
                E("T", lambda e, o=pX.ap, l=lt.ap, r=rt.ap, st=(ch == 0), sp=(ch == C - 1):
                  e.matmul(out=o, lhsT=l, rhs=r, start=st, stop=sp), [lt, rt], [pX])
            dcol = oc * 1024 + (tt0 - h0)
            dt_ = dst[:, dcol:dcol + TT]
            if phi:
                a = P.tmpf[:, :]
                b2 = P.tmpg[:, :]
                E("V", lambda e, o=a.ap, i=pX.ap, b=bc_.ap:
                  e.tensor_scalar(out=o, in0=i, scalar1=b, scalar2=0.0, op0=ALU.add, op1=ALU.min),
                  [pX, bc_], [a])
                E("S", lambda e, o=b2.ap, i=a.ap: e.activation(out=o, in_=i, func=AF.Exp), [a], [b2])
                E("V", lambda e, o=a.ap, i=pX.ap, b=bc_.ap:
                  e.tensor_scalar(out=o, in0=i, scalar1=b, scalar2=0.0, op0=ALU.add, op1=ALU.max),
                  [pX, bc_], [a])
                E("V", lambda e, o=dt_.ap, i=b2.ap, j=a.ap: e.tensor_tensor(out=o, in0=i, in1=j, op=ALU.add),
                  [a, b2], [dt_])
            elif scale is not None:
                E("S", lambda e, o=dt_.ap, i=pX.ap, sc=scale, b=bc_.ap:
                  e.activation(out=o, in_=i, func=AF.Identity, scale=sc, bias=b), [pX, bc_], [dt_])
            else:
                E("V", lambda e, o=dt_.ap, i=pX.ap, b=bc_.ap:
                  e.tensor_scalar(out=o, in0=i, scalar1=b, scalar2=None, op0=ALU.add, op1=ALU.bypass),
                  [pX, bc_], [dt_])


def _proj_tm(P, W, dst, brow, t0, t1):
    E = P.E
    h0 = (t0 // HS) * HS
    for tk0 in range(t0, t1, 128):
        for n in range(2):
            pX = P.bank(4 + ((tk0 // 128) % 2))[:, 0:TT]
            br = brow[0:1, n * 512:(n + 1) * 512]
            E("T", lambda e, o=pX.ap, l=P.ones_r[0:1, 0:128].ap, r=br.ap:
              e.matmul(out=o, lhsT=l, rhs=r, start=True, stop=False), [br, P.ones_r[0:1, 0:128]], [pX])
            for ch in range(C):
                lt = P.fm(P.xh, ch, tk0, tk0 + 128)
                rt = W[:, ch * 1024 + n * 512: ch * 1024 + (n + 1) * 512]
                E("T", lambda e, o=pX.ap, l=lt.ap, r=rt.ap, sp=(ch == C - 1):
                  e.matmul(out=o, lhsT=l, rhs=r, start=False, stop=sp), [lt, rt], [pX])
            dcol = ((tk0 - h0) // 128) * 1024 + n * 512
            dt_ = dst[:, dcol:dcol + TT]
            E("V", lambda e, o=dt_.ap, i=pX.ap: e.tensor_copy(out=o, in_=i), [pX], [dt_])


def _proj_small(P, W, bcol, dst, scale, xsrc):
    E = P.E
    for oc in range(C):
        pX = P.bank(3)[:, 0:8]
        bc_ = bcol[:, oc:oc + 1]
        for ch in range(C):
            lt = W[:, ch * 1024 + oc * 128: ch * 1024 + (oc + 1) * 128]
            rt = xsrc[:, ch * 8:(ch + 1) * 8]
            E("T", lambda e, o=pX.ap, l=lt.ap, r=rt.ap, st=(ch == 0), sp=(ch == C - 1):
              e.matmul(out=o, lhsT=l, rhs=r, start=st, stop=sp), [lt, rt], [pX])
        d_ = dst[:, oc * 8:(oc + 1) * 8]
        if scale is not None:
            E("S", lambda e, o=d_.ap, i=pX.ap, sc=scale, b=bc_.ap:
              e.activation(out=o, in_=i, func=AF.Identity, scale=sc, bias=b), [pX, bc_], [d_])
        else:
            E("V", lambda e, o=d_.ap, i=pX.ap, b=bc_.ap:
              e.tensor_scalar(out=o, in0=i, scalar1=b, scalar2=None, op0=ALU.add, op1=ALU.bypass),
              [pX, bc_], [d_])


def _load_wmat(P, cv, name):
    src = P.din(name, [DM, DM])
    for ch in range(C):
        P.dma("G", cv["W"][:, ch * 1024:(ch + 1) * 1024], APK(src[ch * 128:(ch + 1) * 128, :], []))


def _outproj(P, cv, wname, bname):
    E = P.E
    _load_wmat(P, cv, wname)
    P.dma("G", P.rowA[0:1, 0:1024], APK(P.din(bname, [DM]).unsqueeze(0), []))
    W_ = cv["W"]
    for oc in range(C):
        emit_bcol(P, P.rowA, oc, P.bcA)
        bc_ = P.bcA[:, oc:oc + 1]
        for tt in range(NTT):
            pX = P.bank(oc % 4)[:, 0:TT]
            for ch in range(C):
                lt = W_[:, ch * 1024 + oc * 128: ch * 1024 + (oc + 1) * 128]
                rt = P.fm(P.xh, ch, tt * TT, (tt + 1) * TT)
                E("T", lambda e, o=pX.ap, l=lt.ap, r=rt.ap, st=(ch == 0), sp=(ch == C - 1):
                  e.matmul(out=o, lhsT=l, rhs=r, start=st, stop=sp), [lt, rt], [pX])
            xrt = P.fm(P.xr, oc, tt * TT, (tt + 1) * TT)
            E("V", lambda e, o=xrt.ap, i=pX.ap, b=bc_.ap, x=xrt.ap:
              e.scalar_tensor_tensor(out=o, in0=i, scalar=b, in1=x, op0=ALU.add, op1=ALU.add),
              [pX, bc_, xrt], [xrt])


ATTN0_PARTS = {"qkv", "local", "glob", "comb", "outproj"}
A0LD = 9
A0NB = 8
A0NH = 8
A0NG = 2


def emit_attn0(P):
    E = P.E
    cv = _attn_carves(P)
    K_, Q_, V_, W_ = cv["K"], cv["Q"], cv["V"], cv["W"]
    ssb, png, pT, pTg = cv["ssb"], cv["png"], cv["pT"], cv["pTg"]
    P.dma("G", P.rowA[0:1, 0:1024], APK(P.din("L0_attn_bq", [DM]).unsqueeze(0), []))
    P.dma("G", P.rowB[0:1, 0:1024], APK(P.din("L0_attn_bk", [DM]).unsqueeze(0), []))
    P.dma("G", P.rowC[0:1, 0:1024], APK(P.din("L0_attn_bv", [DM]).unsqueeze(0), []))
    for ch in range(C):
        for j, tg in enumerate(GLOBAL_TOKENS):
            s_ = P.fm(P.xh, ch, tg, tg + 1)
            d_ = cv["xhg"][:, ch * 8 + j:ch * 8 + j + 1]
            E("V", lambda e, o=d_.ap, i=s_.ap: e.tensor_copy(out=o, in_=i), [s_], [d_])

    SC = 1.0 / 8.0
    for half in range(2):
        t0, t1 = half * HS, (half + 1) * HS
        _load_wmat(P, cv, "L0_attn_Wq")
        _proj_fm(P, W_, Q_, P.rowA, t0, t1, scale=SC, bcol=P.bcA, build_cols=(half == 0))
        if half == 0:
            _proj_small(P, W_, P.bcA, cv["qg"], SC, cv["xhg"])
        _load_wmat(P, cv, "L0_attn_Wk")
        _proj_fm(P, W_, K_, P.rowB, t0, t1, bcol=P.bcB, build_cols=(half == 0))
        if half == 0:
            _proj_small(P, W_, P.bcB, cv["kg"], None, cv["xhg"])
        _load_wmat(P, cv, "L0_attn_Wv")
        _proj_tm(P, W_, V_, P.rowC, t0, t1)
        if half == 0:
            for n in range(2):
                pX = P.bank(3)[0:8, 0:TT]
                E("T", lambda e, o=pX.ap, l=P.ones_r[0:1, 0:8].ap, r=P.rowC[0:1, n * 512:(n + 1) * 512].ap:
                  e.matmul(out=o, lhsT=l, rhs=r, start=True, stop=False),
                  [P.rowC[0:1, n * 512:(n + 1) * 512], P.ones_r[0:1, 0:8]], [pX])
                for ch in range(C):
                    lt = cv["xhg"][:, ch * 8:(ch + 1) * 8]
                    rt = W_[:, ch * 1024 + n * 512: ch * 1024 + (n + 1) * 512]
                    E("T", lambda e, o=pX.ap, l=lt.ap, r=rt.ap, sp=(ch == C - 1):
                      e.matmul(out=o, lhsT=l, rhs=r, start=False, stop=sp), [lt, rt], [pX])
                d_ = cv["vg"][0:8, n * 512:(n + 1) * 512]
                E("V", lambda e, o=d_.ap, i=pX.ap: e.tensor_copy(out=o, in_=i), [pX], [d_])

        if half == 0:
            kg2 = cv["kg2"]
            E("V", lambda e, o=kg2[0:128, 0:128].ap: e.memset(o, 0.0), [], [kg2[0:128, 0:128]])
            E("V", lambda e, o=cv["Qz"][0:128, 0:2048].ap: e.memset(o, 0.0), [], [cv["Qz"][0:128, 0:2048]])
            for h in range(NH):
                prow, chh = (h % 2) * 64, h // 2
                s_ = cv["kg"][prow:prow + 64, chh * 8:(chh + 1) * 8]
                d_ = kg2[prow:prow + 64, h * 8:(h + 1) * 8]
                E("V", lambda e, o=d_.ap, i=s_.ap: e.tensor_copy(out=o, in_=i), [s_], [d_])

        def a0_scores(bl, grp):
            if grp == 0:
                for h in range(NH):
                    prow, chh = (h % 2) * 64, h // 2
                    s_ = Q_[prow:prow + 64, chh * 1024 + bl * 128: chh * 1024 + (bl + 1) * 128]
                    d_ = cv["Qz"][prow:prow + 64, h * 128:(h + 1) * 128]
                    E("V", lambda e, o=d_.ap, i=s_.ap: e.tensor_copy(out=o, in_=i), [s_], [d_])
            pl = P.banks(grp * 2, 2)
            for hh in range(8):
                h = grp * 8 + hh
                chh = h // 2
                qsg = Q_[0:128, chh * 1024 + bl * 128: chh * 1024 + (bl + 1) * 128]
                kgs = cv["kg2"][0:128, h * 8:(h + 1) * 8]
                og_ = P.bank(4)[:, h * 8:(h + 1) * 8]
                E("T", lambda e, o=og_.ap, l=qsg.ap, r=kgs.ap:
                  e.matmul(out=o, lhsT=l, rhs=r, start=True, stop=True), [qsg, kgs], [og_])
                qs = cv["Qz"][0:128, h * 128:(h + 1) * 128]
                ks = K_[0:128, chh * 1024 + bl * 128: chh * 1024 + (bl + 1) * 128]
                ol = pl[:, hh * 128:(hh + 1) * 128]
                E("T", lambda e, o=ol.ap, l=qs.ap, r=ks.ap:
                  e.matmul(out=o, lhsT=l, rhs=r, start=True, stop=True), [qs, ks], [ol])

        def a0_softmax(bl, grp):
            b = half * 8 + bl
            gin = [g for g in GLOBAL_TOKENS if b * 128 <= g < (b + 1) * 128]
            hbase = grp * 8
            pl = P.banks(grp * 2, 2)
            ssb = cv["ssb"] if grp == 0 else cv["gp"]
            fo = grp * 40
            E("V", lambda e, o=ssb[:, 0:1024].ap, i=pl[:, 0:1024].ap, m=P.quad[:, 0:1024].ap:
              e.tensor_tensor(out=o, in0=i, in1=m, op=ALU.add),
              [pl[:, 0:1024], P.quad[:, 0:1024]], [ssb[:, 0:1024]])
            for g in gin:
                off = g - b * 128
                for hh in range(8):
                    s_ = pl[:, hh * 128 + off: hh * 128 + off + 1]
                    d_ = ssb[:, hh * 128 + off: hh * 128 + off + 1]
                    E("V", lambda e, o=d_.ap, i=s_.ap: e.tensor_copy(out=o, in_=i), [s_], [d_])
            s3 = ssb[:, 0:1024].r(lambda ap: ap.rearrange("p (a b) -> p a b", b=128))
            gg = P.bank(4)[:, hbase * 8:(hbase + 8) * 8]
            gg3 = gg.r(lambda ap: ap.rearrange("p (a b) -> p a b", b=8))
            mx1 = P.fstat[:, fo + 0:fo + 8]
            E("V", lambda e, o=mx1.ap, i=s3.ap: e.tensor_reduce(out=o, in_=i, axis=AX.X, op=ALU.max),
              [ssb[:, 0:1024]], [mx1])
            mx2 = P.fstat[:, fo + 8:fo + 16]
            E("V", lambda e, o=mx2.ap, i=gg3.ap: e.tensor_reduce(out=o, in_=i, axis=AX.X, op=ALU.max),
              [gg], [mx2])
            E("V", lambda e, o=mx1.ap, i=mx1.ap, j=mx2.ap: e.tensor_tensor(out=o, in0=i, in1=j, op=ALU.max),
              [mx1, mx2], [mx1])
            E("V", lambda e, o=s3.ap, i=s3.ap, m=mx1.ap.unsqueeze(-1).broadcast_to((128, 8, 128)):
              e.tensor_tensor(out=o, in0=i, in1=m, op=ALU.subtract), [ssb[:, 0:1024], mx1], [ssb[:, 0:1024]])
            pn_g = png[:, hbase * 8:(hbase + 8) * 8]
            pn_g3 = pn_g.r(lambda ap: ap.rearrange("p (a b) -> p a b", b=8))
            E("V", lambda e, o=pn_g3.ap, i=gg3.ap, m=mx1.ap.unsqueeze(-1).broadcast_to((128, 8, 8)):
              e.tensor_tensor(out=o, in0=i, in1=m, op=ALU.subtract), [gg, mx1], [pn_g])
            E("S", lambda e, o=ssb[:, 0:1024].ap, i=ssb[:, 0:1024].ap:
              e.activation(out=o, in_=i, func=AF.Exp), [ssb[:, 0:1024]], [ssb[:, 0:1024]])
            E("S", lambda e, o=pn_g.ap, i=pn_g.ap: e.activation(out=o, in_=i, func=AF.Exp),
              [pn_g], [pn_g])
            for g in gin:
                gi = GLOBAL_TOKENS.index(g)
                for hh in range(8):
                    d_ = png[:, (hbase + hh) * 8 + gi:(hbase + hh) * 8 + gi + 1]
                    E("V", lambda e, o=d_.ap: e.memset(o, 0.0), [], [d_])
            s1 = P.fstat[:, fo + 16:fo + 24]
            E("V", lambda e, o=s1.ap, i=s3.ap: e.tensor_reduce(out=o, in_=i, axis=AX.X, op=ALU.add),
              [ssb[:, 0:1024]], [s1])
            s2 = P.fstat[:, fo + 24:fo + 32]
            E("V", lambda e, o=s2.ap, i=pn_g3.ap: e.tensor_reduce(out=o, in_=i, axis=AX.X, op=ALU.add),
              [pn_g], [s2])
            E("V", lambda e, o=s1.ap, i=s1.ap, j=s2.ap: e.tensor_tensor(out=o, in0=i, in1=j, op=ALU.add),
              [s1, s2], [s1])
            rc = P.fstat[:, fo + 32:fo + 40]
            E("V", lambda e, o=rc.ap, i=s1.ap: e.reciprocal(out=o, in_=i), [s1], [rc])
            E("V", lambda e, o=s3.ap, i=s3.ap, m=rc.ap.unsqueeze(-1).broadcast_to((128, 8, 128)):
              e.tensor_tensor(out=o, in0=i, in1=m, op=ALU.mult), [ssb[:, 0:1024], rc], [ssb[:, 0:1024]])
            E("V", lambda e, o=pn_g3.ap, i=pn_g3.ap, m=rc.ap.unsqueeze(-1).broadcast_to((128, 8, 8)):
              e.tensor_tensor(out=o, in0=i, in1=m, op=ALU.mult), [pn_g, rc], [pn_g])

        def a0_tav(bl, grp):
            hbase = grp * 8
            ssb = cv["ssb"] if grp == 0 else cv["gp"]
            for hh in range(8):
                h = hbase + hh
                prow, chh = (h % 2) * 64, h // 2
                ps = ssb[:, hh * 128:(hh + 1) * 128]
                ptp = P.bank(5, BF16)[0:128, (hh % 4) * 128:((hh % 4) + 1) * 128]
                E("T", lambda e, o=ptp.ap, i=ps.ap, idm=P.idbf[:, :].ap:
                  e.transpose(out=o, in_=i, identity=idm), [ps, P.idbf[:, :]], [ptp])
                d_ = pT[:, hh * 128:(hh + 1) * 128]
                E("V", lambda e, o=d_.ap, i=ptp.ap: e.tensor_copy(out=o, in_=i), [ptp], [d_])
                pgs = png[:, h * 8:(h + 1) * 8]
                ptg = P.bank(5, BF16)[0:8, 512 + (hh % 4) * 128: 512 + ((hh % 4) + 1) * 128]
                E("T", lambda e, o=ptg.ap, i=pgs.ap, idm=P.idbf[:, :].ap:
                  e.transpose(out=o, in_=i, identity=idm), [pgs, P.idbf[:, :]], [ptg])
                dg_ = pTg[0:8, hh * 128:(hh + 1) * 128]
                E("V", lambda e, o=dg_.ap, i=ptg.ap: e.tensor_copy(out=o, in_=i), [ptg], [dg_])
                po = P.bank(6 + grp)[0:64, (hh % 4) * 128:((hh % 4) + 1) * 128]
                vloc = V_[:, bl * 1024 + h * 64: bl * 1024 + (h + 1) * 64]
                E("T", lambda e, o=po.ap, l=vloc.ap, r=d_.ap:
                  e.matmul(out=o, lhsT=l, rhs=r, start=True, stop=False), [vloc, d_], [po])
                vgs = cv["vg"][0:8, h * 64:(h + 1) * 64]
                E("T", lambda e, o=po.ap, l=vgs.ap, r=dg_.ap:
                  e.matmul(out=o, lhsT=l, rhs=r, start=False, stop=True), [vgs, dg_], [po])
                ao = P.xh[prow:prow + 64,
                          chh * S + half * HS + bl * 128: chh * S + half * HS + (bl + 1) * 128]
                E("V", lambda e, o=ao.ap, i=po.ap: e.tensor_copy(out=o, in_=i), [po], [ao])

        if "local" in ATTN0_PARTS:
            prev = None
            for bl in range(8):
                for grp in range(2):
                    a0_scores(bl, grp)
                    a0_softmax(bl, grp)
                    if prev is not None:
                        a0_tav(*prev)
                    prev = (bl, grp)
            a0_tav(*prev)

        og = cv["og0"] if half == 0 else cv["og1"]
        for grp in (range(6) if "glob" in ATTN0_PARTS else []):
            heads = [3 * grp + j for j in range(3) if 3 * grp + j < NH]
            psc = P.banks(0, 2)
            for j, h in enumerate(heads):
                prow, chh = (h % 2) * 64, h // 2
                qgs = cv["qg"][prow:prow + 64, chh * 8:(chh + 1) * 8]
                for n in range(2):
                    ksl = K_[prow:prow + 64, chh * 1024 + n * 512: chh * 1024 + (n + 1) * 512]
                    o_ = psc[32 * j:32 * j + 8, n * 512:(n + 1) * 512]
                    E("T", lambda e, o=o_.ap, l=qgs.ap, r=ksl.ap:
                      e.matmul(out=o, lhsT=l, rhs=r, start=True, stop=True), [qgs, ksl], [o_])
            gm = P.fstat[:, 40:41]
            E("V", lambda e, o=gm.ap, i=psc[:, 0:1024].ap:
              e.tensor_reduce(out=o, in_=i, axis=AX.X, op=ALU.max), [psc[:, 0:1024]], [gm])
            ngm = P.fstat[:, 41:42]
            E("V", lambda e, o=ngm.ap, i=gm.ap: e.tensor_single_scalar(out=o, in_=i, scalar=-1.0, op=ALU.mult),
              [gm], [ngm])
            gpv = cv["gp"]
            gpb = gpv[:, 0:1024]
            E("S", lambda e, o=gpb.ap, i=psc[:, 0:1024].ap, b=ngm.ap:
              e.activation(out=o, in_=i, func=AF.Exp, bias=b), [psc[:, 0:1024], ngm], [gpb])
            gs = P.fstat[:, 42:43]
            E("V", lambda e, o=gs.ap, i=gpb.ap: e.tensor_reduce(out=o, in_=i, axis=AX.X, op=ALU.add),
              [gpb], [gs])
            for (srccol, dstcol) in ((40, grp * 4 + half * 2), (42, grp * 4 + half * 2 + 1)):
                s_ = P.fstat[:, srccol:srccol + 1]
                d_ = cv["gst"][:, dstcol:dstcol + 1]
                E("V", lambda e, o=d_.ap, i=s_.ap: e.tensor_copy(out=o, in_=i), [s_], [d_])
            for kc in range(8):
                ptp = P.bank(5, BF16)[0:128, (kc % 4) * 128:((kc % 4) + 1) * 128]
                s_ = gpv[:, kc * 128:(kc + 1) * 128]
                E("T", lambda e, o=ptp.ap, i=s_.ap, idm=P.idbf[:, :].ap:
                  e.transpose(out=o, in_=i, identity=idm), [s_, P.idbf[:, :]], [ptp])
                d_ = pT[:, kc * 128:(kc + 1) * 128]
                E("V", lambda e, o=d_.ap, i=ptp.ap: e.tensor_copy(out=o, in_=i), [ptp], [d_])
            for j, h in enumerate(heads):
                po = P.bank(3)[0:64, h * 8:(h + 1) * 8]
                for kc in range(8):
                    vsl = V_[:, kc * 1024 + h * 64: kc * 1024 + (h + 1) * 64]
                    dk = pT[:, kc * 128:(kc + 1) * 128]
                    rk = APK(dk.ap[:, 32 * j:32 * j + 8], dk.keys)
                    E("T", lambda e, o=po.ap, l=vsl.ap, r=rk.ap, st=(kc == 0), sp=(kc == 7):
                      e.matmul(out=o, lhsT=l, rhs=r, start=st, stop=sp), [vsl, rk], [po])
            for h in heads:
                po = P.bank(3)[0:64, h * 8:(h + 1) * 8]
                d_ = og[0:64, h * 8:(h + 1) * 8]
                E("V", lambda e, o=d_.ap, i=po.ap: e.tensor_copy(out=o, in_=i), [po], [d_])

    gst = cv["gst"]
    def vv(o, i, j, op):
        E("V", lambda e, o_=o.ap, i_=i.ap, j_=j.ap: e.tensor_tensor(out=o_, in0=i_, in1=j_, op=op),
          [i, j], [o])
    for grp in (range(6) if "comb" in ATTN0_PARTS else []):
        heads = [3 * grp + j for j in range(3) if 3 * grp + j < NH]
        g4 = grp * 4
        m0, s0, m1, s1 = gst[:, g4:g4 + 1], gst[:, g4 + 1:g4 + 2], gst[:, g4 + 2:g4 + 3], gst[:, g4 + 3:g4 + 4]
        M, nM = gst[:, 24:25], gst[:, 25:26]
        f0, f1 = gst[:, 26:27], gst[:, 27:28]
        den, t2_, rc = gst[:, 28:29], gst[:, 29:30], gst[:, 30:31]
        c0, c1 = gst[:, 31:32], gst[:, 32:33]
        vv(M, m0, m1, ALU.max)
        E("V", lambda e, o=nM.ap, i=M.ap: e.tensor_single_scalar(out=o, in_=i, scalar=-1.0, op=ALU.mult),
          [M], [nM])
        E("S", lambda e, o=f0.ap, i=m0.ap, b=nM.ap: e.activation(out=o, in_=i, func=AF.Exp, bias=b),
          [m0, nM], [f0])
        E("S", lambda e, o=f1.ap, i=m1.ap, b=nM.ap: e.activation(out=o, in_=i, func=AF.Exp, bias=b),
          [m1, nM], [f1])
        vv(den, s0, f0, ALU.mult)
        vv(t2_, s1, f1, ALU.mult)
        vv(den, den, t2_, ALU.add)
        E("V", lambda e, o=rc.ap, i=den.ap: e.reciprocal(out=o, in_=i), [den], [rc])
        vv(c0, f0, rc, ALU.mult)
        vv(c1, f1, rc, ALU.mult)
        for (csrc, coff) in ((c0, 0), (c1, 128)):
            cbf = P.tmp1[:, 0:1]
            E("V", lambda e, o=cbf.ap, i=csrc.ap: e.tensor_copy(out=o, in_=i), [csrc], [cbf])
            ptp = P.bank(5, BF16)[0:1, 0:128]
            E("T", lambda e, o=ptp.ap, i=cbf.ap, idm=P.idbf[:, :].ap:
              e.transpose(out=o, in_=i, identity=idm), [cbf, P.idbf[:, :]], [ptp])
            rowt = P.tmp2[0:1, 0:128]
            E("V", lambda e, o=rowt.ap, i=ptp.ap: e.tensor_copy(out=o, in_=i), [ptp], [rowt])
            for j, h in enumerate(heads):
                s_ = P.tmp2[0:1, 32 * j:32 * j + 8]
                d_ = cv["crow"][0:1, coff + h * 8:coff + (h + 1) * 8]
                E("V", lambda e, o=d_.ap, i=s_.ap: e.tensor_copy(out=o, in_=i), [s_], [d_])
    for (coff, cdst) in (((0, cv["cb0"]), (128, cv["cb1"])) if "comb" in ATTN0_PARTS else ()):
        crow = cv["crow"][0:1, coff:coff + 128]
        pb = P.bank(3)[0:64, 128:256]
        E("T", lambda e, o=pb.ap, l=P.ones_r[0:1, 0:64].ap, r=crow.ap:
          e.matmul(out=o, lhsT=l, rhs=r, start=True, stop=True), [crow, P.ones_r[0:1, 0:64]], [pb])
        E("V", lambda e, o=cdst[0:64, 0:128].ap, i=pb.ap: e.tensor_copy(out=o, in_=i), [pb], [cdst[0:64, 0:128]])
    ogf = cv["ogf"]
    if "comb" not in ATTN0_PARTS:
        return
    E("V", lambda e, o=ogf[0:64, 0:128].ap, i=cv["og0"][0:64, 0:128].ap, j=cv["cb0"][0:64, 0:128].ap:
      e.tensor_tensor(out=o, in0=i, in1=j, op=ALU.mult),
      [cv["og0"][0:64, 0:128], cv["cb0"][0:64, 0:128]], [ogf[0:64, 0:128]])
    E("V", lambda e, o=cv["og0"][0:64, 0:128].ap, i=cv["og1"][0:64, 0:128].ap, j=cv["cb1"][0:64, 0:128].ap:
      e.tensor_tensor(out=o, in0=i, in1=j, op=ALU.mult),
      [cv["og1"][0:64, 0:128], cv["cb1"][0:64, 0:128]], [cv["og0"][0:64, 0:128]])
    E("V", lambda e, o=ogf[0:64, 0:128].ap, i=ogf[0:64, 0:128].ap, j=cv["og0"][0:64, 0:128].ap:
      e.tensor_tensor(out=o, in0=i, in1=j, op=ALU.add),
      [ogf[0:64, 0:128], cv["og0"][0:64, 0:128]], [ogf[0:64, 0:128]])
    for h in (range(NH) if "comb" in ATTN0_PARTS else []):
        prow, chh = (h % 2) * 64, h // 2
        for gi, tg in enumerate(GLOBAL_TOKENS):
            s_ = ogf[0:64, h * 8 + gi:h * 8 + gi + 1]
            d_ = P.xh[prow:prow + 64, chh * S + tg: chh * S + tg + 1]
            E("V", lambda e, o=d_.ap, i=s_.ap: e.tensor_copy(out=o, in_=i), [s_], [d_])

    if "outproj" in ATTN0_PARTS:
        _outproj(P, cv, "L0_attn_Wo", "L0_attn_bo")


def emit_attn1(P):
    E = P.E
    cv = _attn_carves(P)
    K_, Q_, V_, W_ = cv["K"], cv["Q"], cv["V"], cv["W"]
    KVs, am, ktm, rr, rb = cv["KVs"], cv["am"], cv["ktm"], cv["rr"], cv["rb"]
    KVb = cv["KVb"]
    P.dma("G", P.rowA[0:1, 0:1024], APK(P.din("L1_attn_bq", [DM]).unsqueeze(0), []))
    P.dma("G", P.rowB[0:1, 0:1024], APK(P.din("L1_attn_bk", [DM]).unsqueeze(0), []))
    P.dma("G", P.rowC[0:1, 0:1024], APK(P.din("L1_attn_bv", [DM]).unsqueeze(0), []))
    E("V", lambda e, o=KVs[0:128, 0:16 * 65].ap: e.memset(o, 0.0), [], [KVs[0:128, 0:16 * 65]])
    E("V", lambda e, o=KVb[0:128, 0:16 * 65].ap: e.memset(o, 0.0), [], [KVb[0:128, 0:16 * 65]])

    for half in range(2):
        t0, t1 = half * HS, (half + 1) * HS
        _load_wmat(P, cv, "L1_attn_Wq")
        _proj_fm(P, W_, Q_, P.rowA, t0, t1, phi=True, bcol=P.bcA, build_cols=(half == 0))
        _load_wmat(P, cv, "L1_attn_Wk")
        _proj_fm(P, W_, K_, P.rowB, t0, t1, phi=True, bcol=P.bcB, build_cols=(half == 0))
        _load_wmat(P, cv, "L1_attn_Wv")
        _proj_tm(P, W_, V_, P.rowC, t0, t1)

        for cl in range(8):
            cg = half * 8 + cl
            for hg in range(4):
                hs = []
                for j in range(4):
                    h = hg * 4 + j
                    prow, chh = (h % 2) * 64, h // 2
                    kf = K_[prow:prow + 64, chh * 1024 + cl * 128: chh * 1024 + (cl + 1) * 128]
                    qf = Q_[prow:prow + 64, chh * 1024 + cl * 128: chh * 1024 + (cl + 1) * 128]
                    vt = V_[:, cl * 1024 + h * 64: cl * 1024 + (h + 1) * 64]
                    ams = am[:, j * 128:(j + 1) * 128]
                    kts = ktm[:, j * 64:(j + 1) * 64]
                    hs.append((h, prow, kf, qf, vt, ams, kts))
                for (h, prow, kf, qf, vt, ams, kts) in hs:
                    j = h % 4
                    pa = P.bank(0)[:, j * 128:(j + 1) * 128]
                    E("T", lambda e, o=pa.ap, l=kf.ap, r=qf.ap:
                      e.matmul(out=o, lhsT=l, rhs=r, start=True, stop=True), [kf, qf], [pa])
                    pt = P.bank(5, BF16)[0:128, j * 64:(j + 1) * 64]
                    idsl = P.idbf[prow:prow + 64, prow:prow + 64]
                    E("T", lambda e, o=pt.ap, i=kf.ap, idm=idsl.ap:
                      e.transpose(out=o, in_=i, identity=idm), [kf, idsl], [pt])
                    E("V", lambda e, o=ams.ap, i=pa.ap, m=P.tri[:, :].ap:
                      e.tensor_tensor(out=o, in0=i, in1=m, op=ALU.mult), [pa, P.tri[:, :]], [ams])
                    E("V", lambda e, o=kts.ap, i=pt.ap: e.tensor_copy(out=o, in_=i), [pt], [kts])
                for (h, prow, kf, qf, vt, ams, kts) in hs:
                    j = h % 4
                    pnum = P.bank(1)[0:64, j * 128:(j + 1) * 128]
                    pden = P.bank(4)[0:1, j * 128:(j + 1) * 128]
                    kv64 = KVb[prow:prow + 64, h * 65: h * 65 + 64]
                    ks1 = KVb[prow:prow + 64, h * 65 + 64: h * 65 + 65]
                    E("T", lambda e, o=pnum.ap, l=kv64.ap, r=qf.ap:
                      e.matmul(out=o, lhsT=l, rhs=r, start=True, stop=False), [kv64, qf], [pnum])
                    E("T", lambda e, o=pnum.ap, l=vt.ap, r=ams.ap:
                      e.matmul(out=o, lhsT=l, rhs=r, start=False, stop=True), [vt, ams], [pnum])
                    E("T", lambda e, o=pden.ap, l=ks1.ap, r=qf.ap:
                      e.matmul(out=o, lhsT=l, rhs=r, start=True, stop=False), [ks1, qf], [pden])
                    E("T", lambda e, o=pden.ap, l=P.ones_c[:, 0:1].ap, r=ams.ap:
                      e.matmul(out=o, lhsT=l, rhs=r, start=False, stop=True),
                      [P.ones_c[:, 0:1], ams], [pden])
                    sl = ((h // 2) % 4) * 66
                    kvp = P.bank(6)[prow:prow + 64, sl:sl + 64]
                    E("T", lambda e, o=kvp.ap, l=kts.ap, r=vt.ap:
                      e.matmul(out=o, lhsT=l, rhs=r, start=True, stop=True), [kts, vt], [kvp])
                    ksp = P.bank(6)[prow:prow + 64, sl + 64:sl + 65]
                    E("T", lambda e, o=ksp.ap, l=kts.ap, r=P.ones_c[:, 0:1].ap:
                      e.matmul(out=o, lhsT=l, rhs=r, start=True, stop=True),
                      [kts, P.ones_c[:, 0:1]], [ksp])
                    dr = rr[0:1, j * 128:(j + 1) * 128]
                    E("V", lambda e, o=dr.ap, i=pden.ap:
                      e.tensor_single_scalar(out=o, in_=i, scalar=1e-6, op=ALU.add), [pden], [dr])
                    E("V", lambda e, o=dr.ap, i=dr.ap: e.reciprocal(out=o, in_=i), [dr], [dr])
                    kvf = KVs[prow:prow + 64, h * 65: h * 65 + 65]
                    kvpf = P.bank(6)[prow:prow + 64, sl:sl + 65]
                    E("V", lambda e, o=kvf.ap, i=kvpf.ap, x=kvf.ap:
                      e.tensor_tensor(out=o, in0=i, in1=x, op=ALU.add), [kvpf, kvf], [kvf])
                    kvb = KVb[prow:prow + 64, h * 65: h * 65 + 65]
                    E("V", lambda e, o=kvb.ap, i=kvf.ap: e.tensor_copy(out=o, in_=i), [kvf], [kvb])
                rrs = rr[0:1, 0:512]
                pb = P.bank(3)[0:64, 0:512]
                E("T", lambda e, o=pb.ap, l=P.ones_f[0:1, 0:64].ap, r=rrs.ap:
                  e.matmul(out=o, lhsT=l, rhs=r, start=True, stop=True), [rrs, P.ones_f[0:1, 0:64]], [pb])
                E("V", lambda e, o=rb[0:64, 0:512].ap, i=pb.ap: e.tensor_copy(out=o, in_=i),
                  [pb], [rb[0:64, 0:512]])
                for j in range(4):
                    h = hg * 4 + j
                    prow, chh = (h % 2) * 64, h // 2
                    pnum = P.bank(1)[0:64, j * 128:(j + 1) * 128]
                    rbs = rb[0:64, j * 128:(j + 1) * 128]
                    ao = P.xh[prow:prow + 64,
                              chh * S + half * HS + cl * 128: chh * S + half * HS + (cl + 1) * 128]
                    E("V", lambda e, o=ao.ap, i=pnum.ap, j2=rbs.ap:
                      e.tensor_tensor(out=o, in0=i, in1=j2, op=ALU.mult), [pnum, rbs], [ao])

    _outproj(P, cv, "L1_attn_Wo", "L1_attn_bo")



# version 16 (final)
# speedup vs baseline: 1.2385x; 1.2385x over previous
import math
import numpy as np
import ml_dtypes
import concourse.bass as bass
import concourse.mybir as mybir
from concourse.bass_utils import run_bass_kernel_spmd

F32 = mybir.dt.float32
BF16 = mybir.dt.bfloat16
U8 = mybir.dt.uint8
AF = mybir.ActivationFunctionType
ALU = mybir.AluOpType
AX = mybir.AxisListType

S = 2048
DM = 1024
C = 8
TT = 512
NTT = S // TT
NH = 16
HD = 64
HS = 1024
NEG = -1.0e30

GLOBAL_TOKENS = [0, 292, 584, 877, 1169, 1462, 1754, 2047]

ENGS = ("T", "V", "S", "G", "QS")
import os as _os
STRICT_SYNC = not bool(_os.environ.get("MK_NO_STRICT_SYNC"))
BLOCK_ATTR = {"T": "tensor", "V": "vector", "S": "scalar", "G": "gpsimd", "QS": "sync"}


class APK:
    __slots__ = ("ap", "keys")

    def __init__(self, ap, keys):
        self.ap = ap
        self.keys = keys

    def r(self, fn):
        return APK(fn(self.ap), self.keys)


NQSLOT = 4


class Op:
    __slots__ = ("fn", "kind", "waits", "mark", "clock", "slot")

    def __init__(self, fn, kind, waits, clock, slot=None):
        self.fn = fn
        self.kind = kind
        self.waits = waits
        self.mark = False
        self.clock = clock
        self.slot = slot


class Em:
    def __init__(self):
        self.streams = {e: [] for e in ENGS}
        self.res = {}
        self.clock = {e: {} for e in ENGS}
        self.self_waited = {e: -1 for e in ENGS}
        self.dma_ctr = {e: 0 for e in ENGS}

    def ckey(self, de, di):
        op = self.streams[de][di]
        return de if op.kind == "c" else f"{de}#{op.slot}"

    def emit(self, eng, fn, R=(), W=(), kind="c", force=False):
        rkeys = [k for x in R for k in (x.keys if isinstance(x, APK) else (x,))]
        wkeys = [k for x in W for k in (x.keys if isinstance(x, APK) else (x,))]
        deps = []
        for k in rkeys:
            st = self.res.get(k)
            if st:
                deps.extend(st[0])
        for k in wkeys:
            st = self.res.get(k)
            if st:
                deps.extend(st[0])
                deps.extend(st[1])
        myclock = dict(self.clock[eng])
        waits = []
        seen = set()
        slot = None
        if kind == "d":
            slot = self.dma_ctr[eng] % NQSLOT
            self.dma_ctr[eng] += 1
        for de, di in deps:
            if (de, di) in seen:
                continue
            seen.add((de, di))
            pkind = self.streams[de][di].kind
            if de == eng and pkind == "c" and kind == "c":
                if eng == "T":
                    continue
                is_raw = any(k in rkeys and (de, di) in self.res.get(k, [[], []])[0] for k in rkeys)
                if not is_raw and not force and not STRICT_SYNC:
                    continue
                if di <= self.self_waited[eng]:
                    continue
                waits.append((de, di))
                self.self_waited[eng] = di
                self.streams[de][di].mark = True
                continue
            ck = self.ckey(de, di)
            if myclock.get(ck, -1) >= di:
                continue
            waits.append((de, di))
        for de, di in waits:
            oc = self.streams[de][di].clock
            for k2, v2 in oc.items():
                if myclock.get(k2, -1) < v2:
                    myclock[k2] = v2
            ck = self.ckey(de, di)
            if myclock.get(ck, -1) < di:
                myclock[ck] = di
            self.streams[de][di].mark = True
        idx = len(self.streams[eng])
        myclock[eng if kind == "c" else f"{eng}#{slot}"] = idx
        op = Op(fn, kind, waits, myclock, slot)
        self.streams[eng].append(op)
        self.clock[eng] = myclock
        for k in rkeys:
            self.res.setdefault(k, [[], []])[1].append((eng, idx))
        for k in wkeys:
            st = self.res.setdefault(k, [[], []])
            if kind == "c":
                st[0] = [w for w in st[0]
                         if not (w[0] == eng and self.streams[w[0]][w[1]].kind == "c")]
            st[0].append((eng, idx))
            st[1] = []
        return (eng, idx)


class TV:

    __slots__ = ("ap0", "space", "base", "esz", "pg", "cols", "parts")

    def __init__(self, ap0, space, base_byte, elem_bytes, cols, parts, pg=2048):
        self.ap0 = ap0
        self.space = space
        self.base = base_byte
        self.esz = elem_bytes
        self.pg = pg
        self.cols = cols
        self.parts = parts

    def __getitem__(self, idx):
        ps, cs = idx
        if isinstance(cs, slice):
            c0 = cs.start or 0
            c1 = cs.stop if cs.stop is not None else self.cols
        else:
            c0, c1 = cs, cs + 1
        b0 = self.base + c0 * self.esz
        b1 = self.base + c1 * self.esz
        keys = [f"{self.space}:{p}" for p in range(b0 // self.pg, (b1 - 1) // self.pg + 1)]
        return APK(self.ap0[ps, cs], keys)


def _silu(x):
    return x / (1 + np.exp(-x))


class Prog:

    def __init__(self, phases, name="mk"):
        self.phases = phases
        self.em = Em()
        nc = bass.Bass("TRN2", target_bir_lowering=False, debug=False)
        self.nc = nc
        self._ctxs = []
        self.in_names = set()

        self.d_x = nc.dram_tensor("x", [C, 128, S], F32, kind="ExternalInput").ap()
        self.d_y = nc.dram_tensor("y", [C, 128, S], F32, kind="ExternalOutput").ap()
        self.in_names.add("x")
        self.dram = {}

        def sbt(nm, shape, dt):
            t = self._enter(nc.sbuf_tensor(nm, shape, dt))
            return t

        self.t_xr = sbt("xr", [128, C * S], F32)
        self.xr = TV(self.t_xr[:, :], "xr", 0, 4, C * S, 128)
        self.t_xh = sbt("xh", [128, C * S], BF16)
        self.xh = TV(self.t_xh[:, :], "xh", 0, 2, C * S, 128)

        BIGB = 90624
        self.t_big = sbt("big", [128, BIGB], U8)
        self.bigb = BIGB

        self.t_rowA = sbt("rowA", [1, 1408], BF16)
        self.rowA = TV(self.t_rowA[:, :], "rowA", 0, 2, 1408, 1)
        self.t_rowB = sbt("rowB", [1, 1408], BF16)
        self.rowB = TV(self.t_rowB[:, :], "rowB", 0, 2, 1408, 1)
        self.t_rowC = sbt("rowC", [1, 1024], BF16)
        self.rowC = TV(self.t_rowC[:, :], "rowC", 0, 2, 1024, 1)
        self.t_ones_r = sbt("ones_r", [1, 512], BF16)
        self.ones_r = TV(self.t_ones_r[:, :], "ones_r", 0, 2, 512, 1)
        self.t_ones_c = sbt("ones_c", [128, 1], BF16)
        self.ones_c = TV(self.t_ones_c[:, :], "ones_c", 0, 2, 1, 128)
        self.t_ones_f = sbt("ones_f", [1, 128], F32)
        self.ones_f = TV(self.t_ones_f[:, :], "ones_f", 0, 4, 128, 1)
        self.t_mrow = sbt("mrow", [1, 512], F32)
        self.mrow = TV(self.t_mrow[:, :], "mrow", 0, 4, 512, 1)
        self.t_zrow = sbt("zrow", [1, 512], F32)
        self.zrow = TV(self.t_zrow[:, :], "zrow", 0, 4, 512, 1)
        self.t_tmp1 = sbt("tmp1", [128, 512], BF16)
        self.tmp1 = TV(self.t_tmp1[:, :], "tmp1", 0, 2, 512, 128)
        self.t_tmp2 = sbt("tmp2", [128, 512], BF16)
        self.tmp2 = TV(self.t_tmp2[:, :], "tmp2", 0, 2, 512, 128)
        self.t_tmpf = sbt("tmpf", [128, 512], F32)
        self.tmpf = TV(self.t_tmpf[:, :], "tmpf", 0, 4, 512, 128)
        self.t_tmpf2 = sbt("tmpf2", [128, 512], F32)
        self.tmpf2 = TV(self.t_tmpf2[:, :], "tmpf2", 0, 4, 512, 128)
        self.t_fstat = sbt("fstat", [128, 96], F32)
        self.fstat = TV(self.t_fstat[:, :], "fstat", 0, 4, 96, 128)
        self.t_fscr = sbt("fscr", [128, 128], F32)
        self.fscr = TV(self.t_fscr[:, :], "fscr", 0, 4, 128, 128)
        self.t_probs = sbt("probs", [128, 128], F32)
        self.probs = TV(self.t_probs[:, :], "probs", 0, 4, 128, 128)
        self.t_eps = sbt("epsc", [1, 8], F32)
        self.epsc = TV(self.t_eps[:, :], "epsc", 0, 4, 8, 1)
        self.t_bcA = sbt("bcA", [128, 12], F32)
        self.bcA = TV(self.t_bcA[:, :], "bcA", 0, 4, 12, 128)
        self.t_bcB = sbt("bcB", [128, 12], F32)
        self.bcB = TV(self.t_bcB[:, :], "bcB", 0, 4, 12, 128)
        self.t_bcC = sbt("bcC", [128, 12], F32)
        self.bcC = TV(self.t_bcC[:, :], "bcC", 0, 4, 12, 128)
        self.t_idbf = sbt("idbf", [128, 128], BF16)
        self.idbf = TV(self.t_idbf[:, :], "idbf", 0, 2, 128, 128)
        self.t_tri = sbt("tri", [128, 128], BF16)
        self.tri = TV(self.t_tri[:, :], "tri", 0, 2, 128, 128)
        self.t_quad = sbt("quad", [128, 1024], BF16)
        self.quad = TV(self.t_quad[:, :], "quad", 0, 2, 1024, 128)

        self.t_ps = self._enter(nc.psum_tensor("ps", [128, 4096], F32))
        self.ps = TV(self.t_ps[:, :], "ps", 0, 4, 4096, 128)
        self.psb = TV(self.t_ps[:, :].bitcast(BF16), "ps", 0, 2, 8192, 128)

        self.NQ = 4
        self.sems = {e: self._enter(nc.semaphore("sem" + e)) for e in ENGS}
        self.qsems = {e: [self._enter(nc.semaphore(f"q{e}{i}")) for i in range(self.NQ)]
                      for e in ("G", "QS")}

        for nm, shape in [("c_id", [128, 128]), ("c_tri", [128, 128]), ("c_quad", [128, 1024]),
                          ("c_ones_r", [1, 512]), ("c_ones_c", [128, 1])]:
            self.dram[nm] = nc.dram_tensor(nm, shape, BF16, kind="ExternalInput").ap()
            self.in_names.add(nm)
        self.dram["c_ones_f"] = nc.dram_tensor("c_ones_f", [1, 128], F32, kind="ExternalInput").ap()
        self.in_names.add("c_ones_f")
        self.dram["c_eps"] = nc.dram_tensor("c_eps", [1, 8], F32, kind="ExternalInput").ap()
        self.in_names.add("c_eps")

    def _enter(self, cm):
        v = cm.__enter__()
        self._ctxs.append(cm)
        return v

    def din(self, name, shape):
        if name not in self.dram:
            self.dram[name] = self.nc.dram_tensor(name, shape, F32, kind="ExternalInput").ap()
            self.in_names.add(name)
        return self.dram[name]

    def carve(self, byte_off, parts, cols, dt):
        esz = 2 if dt == BF16 else 4
        nbytes = cols * esz
        assert byte_off % 4 == 0 and byte_off + nbytes <= self.bigb, (byte_off, nbytes)
        ap = self.t_big[0:parts, byte_off:byte_off + nbytes].bitcast(dt)
        return TV(ap, "big", byte_off, esz, cols, parts)

    def bank(self, b, dt=F32):
        if dt == F32:
            return TV(self.t_ps[:, b * 512:(b + 1) * 512], "ps", b * 2048, 4, 512, 128)
        return TV(self.t_ps[:, b * 512:(b + 1) * 512].bitcast(BF16), "ps", b * 2048, 2, 1024, 128)

    def banks(self, b0, nb, dt=F32):
        if dt == F32:
            return TV(self.t_ps[:, b0 * 512:(b0 + nb) * 512], "ps", b0 * 2048, 4, nb * 512, 128)
        return TV(self.t_ps[:, b0 * 512:(b0 + nb) * 512].bitcast(BF16), "ps", b0 * 2048, 2, nb * 1024, 128)

    def E(self, eng, fn, R=(), W=(), kind="c", force=False):
        return self.em.emit(eng, fn, R, W, kind, force)

    def dma(self, q, dst, src, R=(), W=()):
        s_ap = src.ap if isinstance(src, APK) else src
        d_ap = dst.ap if isinstance(dst, APK) else dst
        rr = [src] if isinstance(src, APK) else list(R)
        ww = [dst] if isinstance(dst, APK) else list(W)
        self.em.emit(q, lambda e: e.dma_start(out=d_ap, in_=s_ap), rr + list(R), ww + list(W), kind="d")

    def fm(self, tv, c, t0, t1, stride=S):
        return tv[:, c * stride + t0: c * stride + t1]

    def finalize(self):
        em = self.em
        NQ = self.NQ
        assert NQ == NQSLOT
        self.vals = {}
        self.dslot = {}
        for e in ENGS:
            vs, cnt = [], 0
            slots = [0] * NQ
            dinfo = []
            for op in em.streams[e]:
                if op.kind == "d":
                    sl = op.slot
                    slots[sl] += 1
                    dinfo.append((sl, slots[sl]))
                    vs.append(cnt)
                else:
                    if op.mark:
                        cnt += 1
                    vs.append(cnt)
                    dinfo.append(None)
            self.vals[e] = vs
            self.dslot[e] = dinfo
        nc = self.nc
        sems, qsems, vals, dslot, streams = self.sems, self.qsems, self.vals, self.dslot, em.streams

        def wait_args(de, di):
            if streams[de][di].kind == "d":
                sl, c = dslot[de][di]
                return qsems[de][sl], 16 * c
            return sems[de], vals[de][di]

        with nc.Block() as block:
            for eng in ENGS:
                if not streams[eng]:
                    continue

                def mk(eng):
                    def body(e):
                        sem_c = sems[eng]
                        for i, op in enumerate(streams[eng]):
                            for (de, di) in op.waits:
                                sem, v = wait_args(de, di)
                                e.wait_ge(sem, v)
                            if op.fn is None:
                                continue
                            if op.kind == "d":
                                sl, c = dslot[eng][i]
                                if c > 1:
                                    e.wait_ge(qsems[eng][sl], 16 * (c - 1))
                                inst = op.fn(e)
                                inst.then_inc(qsems[eng][sl], 16)
                            else:
                                inst = op.fn(e)
                                if op.mark:
                                    inst.then_inc(sem_c, 1)
                    return body

                getattr(block, BLOCK_ATTR[eng])(mk(eng))
        return nc



def emit_bcol(P, row, o, dst, scale=None):
    E = P.E
    rs = row[0:1, o * 128:(o + 1) * 128]
    pt = P.bank(7, BF16)[0:128, (o % 4) * 2:(o % 4) * 2 + 1]
    E("T", lambda e, ot=pt.ap, i=rs.ap, idm=P.idbf[0:1, 0:1].ap:
      e.transpose(out=ot, in_=i, identity=idm), [rs, P.idbf[0:1, 0:1]], [pt])
    d_ = dst[:, o:o + 1]
    if scale is None:
        E("V", lambda e, ot=d_.ap, i=pt.ap: e.tensor_copy(out=ot, in_=i), [pt], [d_])
    else:
        E("V", lambda e, ot=d_.ap, i=pt.ap, sc=scale:
          e.tensor_single_scalar(out=ot, in_=i, scalar=sc, op=ALU.mult), [pt], [d_])


def emit_load_x(P):
    for c in range(C):
        P.dma("QS", P.xr[:, c * S:(c + 1) * S], APK(P.d_x[c], []))


def emit_store_x(P):
    for c in range(C):
        src = P.xr[:, c * S:(c + 1) * S]
        P.em.emit("QS", lambda e, s=src.ap, d=P.d_y[c]: e.dma_start(out=d, in_=s),
                  [src], ["dram:y"], kind="d")
    P.em.emit("QS", None, R=["dram:y"], W=())


def emit_consts(P):
    P.dma("QS", APK(P.idbf.ap0, ["idbf:0"]), APK(P.dram["c_id"], []))
    P.dma("QS", APK(P.tri.ap0, ["tri:0"]), APK(P.dram["c_tri"], []))
    P.dma("QS", APK(P.quad.ap0, ["quad:0"]), APK(P.dram["c_quad"], []))
    P.dma("QS", APK(P.ones_r.ap0, ["ones_r:0"]), APK(P.dram["c_ones_r"], []))
    P.dma("QS", APK(P.ones_c.ap0, ["ones_c:0"]), APK(P.dram["c_ones_c"], []))
    P.dma("QS", APK(P.ones_f.ap0, ["ones_f:0"]), APK(P.dram["c_ones_f"], []))
    P.dma("QS", APK(P.epsc.ap0, ["epsc:0"]), APK(P.dram["c_eps"], []))


def emit_ln(P):
    em, E = P.em, P.E
    for tt in range(NTT):
        t0, t1 = tt * TT, (tt + 1) * TT
        for c in range(C):
            xrt = P.fm(P.xr, c, t0, t1)
            a = P.tmp1[:, :]
            b = P.tmp2[:, :]
            E("S", lambda e, o=a.ap, i=xrt.ap: e.copy(out=o, in_=i), [xrt], [a])
            E("S", lambda e, o=b.ap, i=a.ap: e.activation(out=o, in_=i, func=AF.Square), [a], [b])
            psum_s = P.bank(2)[0:1, 0:TT]
            psum_q = P.bank(3)[0:1, 0:TT]
            st = (c == 0)
            sp = (c == C - 1)
            E("T", lambda e, o=psum_s.ap, l=P.ones_c[:, 0:1].ap, r=a.ap, st=st, sp=sp:
              e.matmul(out=o, lhsT=l, rhs=r, start=st, stop=sp), [a, P.ones_c[:, 0:1]], [psum_s])
            E("T", lambda e, o=psum_q.ap, l=P.ones_c[:, 0:1].ap, r=b.ap, st=st, sp=sp:
              e.matmul(out=o, lhsT=l, rhs=r, start=st, stop=sp), [b, P.ones_c[:, 0:1]], [psum_q])
        psum_s = P.bank(2)[0:1, 0:TT]
        psum_q = P.bank(3)[0:1, 0:TT]
        mr = P.mrow[0:1, 0:TT]
        zr = P.zrow[0:1, 0:TT]
        E("V", lambda e, o=mr.ap, i=psum_s.ap: e.tensor_single_scalar(out=o, in_=i, scalar=1.0 / DM,
                                                                     op=ALU.mult), [psum_s], [mr])
        E("V", lambda e, o=zr.ap, i=mr.ap: e.tensor_tensor(out=o, in0=i, in1=i, op=ALU.mult), [mr], [zr])
        E("V", lambda e, o=zr.ap, i=psum_q.ap, s=zr.ap:
          e.scalar_tensor_tensor(out=o, in0=i, scalar=1.0 / DM, in1=s, op0=ALU.mult, op1=ALU.subtract),
          [psum_q, zr], [zr])
        E("S", lambda e, o=zr.ap, i=zr.ap, b=P.epsc[0:1, 0:1].ap:
          e.activation(out=o, in_=i, func=AF.Ln, bias=b), [zr, P.epsc[0:1, 0:1]], [zr])
        E("S", lambda e, o=zr.ap, i=zr.ap: e.activation(out=o, in_=i, func=AF.Exp, scale=-0.5), [zr], [zr])
        mrb = P.mrow[0:1, TT:TT + TT] if False else None
        mb16 = P.tmp1[0:1, 0:TT]
        zb16 = P.tmp2[0:1, 0:TT]
        E("S", lambda e, o=mb16.ap, i=mr.ap: e.copy(out=o, in_=i), [mr], [mb16])
        E("S", lambda e, o=zb16.ap, i=zr.ap: e.copy(out=o, in_=i), [zr], [zb16])
        mbc = P.bank(6)[:, 0:TT]
        zbc = P.bank(7)[:, 0:TT]
        E("T", lambda e, o=mbc.ap, l=P.ones_r[0:1, 0:128].ap, r=mb16.ap:
          e.matmul(out=o, lhsT=l, rhs=r, start=True, stop=True), [mb16, P.ones_r[0:1, 0:128]], [mbc])
        E("T", lambda e, o=zbc.ap, l=P.ones_r[0:1, 0:128].ap, r=zb16.ap:
          e.matmul(out=o, lhsT=l, rhs=r, start=True, stop=True), [zb16, P.ones_r[0:1, 0:128]], [zbc])
        for c in range(C):
            xrt = P.fm(P.xr, c, t0, t1)
            xht = P.fm(P.xh, c, t0, t1)
            tf = P.tmpf[:, :]
            E("V", lambda e, o=tf.ap, i=xrt.ap, m=mbc.ap: e.tensor_tensor(out=o, in0=i, in1=m, op=ALU.subtract),
              [xrt, mbc], [tf])
            E("V", lambda e, o=xht.ap, i=tf.ap, z=zbc.ap: e.tensor_tensor(out=o, in0=i, in1=z, op=ALU.mult),
              [tf, zbc], [xht])



def emit_moe_ffn(P, layer, experts=None):
    em, E = P.em, P.E
    nE = 8
    if layer == 0:
        dh, dhs = 512, 1024
        OC, OCS = dh // 128, dhs // 128
        has_bias = False
        silu_prod = False
        w_names = ("L0_ffn_experts_gate", "L0_ffn_experts_up", "L0_ffn_experts_down")
        s_names = ("L0_ffn_shared_gate", "L0_ffn_shared_up", "L0_ffn_shared_down")
        gate_name, gate_is_he = "L0_ffn_router", True
        b_names = s_bnames = None
        sel_biased = False
    else:
        dh = dhs = 1408
        OC = OCS = 11
        has_bias = True
        silu_prod = True
        w_names = ("L1_ffn_w1", "L1_ffn_w3", "L1_ffn_w2")
        s_names = ("L1_ffn_sw1", "L1_ffn_sw3", "L1_ffn_sw2")
        b_names = ("L1_ffn_b1", "L1_ffn_b3", "L1_ffn_b2")
        s_bnames = ("L1_ffn_sb1", "L1_ffn_sb3", "L1_ffn_sb2")
        gate_name, gate_is_he = "L1_ffn_gate_w", False
        sel_biased = True

    dmax = max(dh, dhs)
    w1v = P.carve(0, 128, C * dmax, BF16)
    w3v = P.carve(22528, 128, C * dmax, BF16)
    w2v = P.carve(45056, 128, (dmax // 128) * 1024, BF16)
    hv = P.carve(67584, 128, (dmax // 128) * TT, BF16)
    cbc = P.carve(79360, 128, 512, F32)
    ctile = P.carve(81408, 8, 2048, BF16)
    coefrow = P.carve(85504, 1, 2048, BF16)
    gwT = P.carve(89600, 128, 64, BF16)
    gbc = P.carve(89728, 128, 8, F32)
    coef_tm = P.carve(89760, 128, 128, BF16)
    stg = P.carve(81408, 8, 1024, BF16)

    if gate_is_he:
        src = P.din(gate_name, [DM, nE])
        for c in range(C):
            P.dma("G", gwT[:, c * 8:(c + 1) * 8], APK(src[c * 128:(c + 1) * 128, :], []))
    else:
        src = P.din(gate_name, [nE, DM])
        P.dma("G", stg[0:8, 0:1024], APK(src, []))
        for c in range(C):
            pt = P.bank(5, BF16)[0:128, 0:8]
            inp = stg[0:8, c * 128:(c + 1) * 128]
            E("T", lambda e, o=pt.ap, i=inp.ap, idm=P.idbf[0:8, 0:8].ap:
              e.transpose(out=o, in_=i, identity=idm), [inp, P.idbf[0:8, 0:8]], [pt])
            dstc = gwT[:, c * 8:(c + 1) * 8]
            E("V", lambda e, o=dstc.ap, i=pt.ap: e.tensor_copy(out=o, in_=i), [pt], [dstc])
        gb = P.din("L1_ffn_gate_b", [nE])
        P.dma("QS", APK(P.mrow.ap0[0:1, 0:8], ["mrow:0"]), APK(gb.unsqueeze(0), []))
        pd = P.bank(3)[:, 0:8]
        E("T", lambda e, o=pd.ap, l=P.ones_f[0:1, :].ap, r=P.mrow[0:1, 0:8].ap:
          e.matmul(out=o, lhsT=l, rhs=r, start=True, stop=True), ["mrow:0", P.ones_f[0:1, :]], [pd])
        E("V", lambda e, o=gbc[:, 0:8].ap, i=pd.ap: e.tensor_copy(out=o, in_=i), [pd], [gbc[:, 0:8]])

    pa = P.bank(0)
    for tc in range(16):
        out = pa[:, tc * 8:(tc + 1) * 8]
        for c in range(C):
            xt = P.fm(P.xh, c, tc * 128, (tc + 1) * 128)
            g = gwT[:, c * 8:(c + 1) * 8]
            E("T", lambda e, o=out.ap, l=xt.ap, r=g.ap, st=(c == 0), sp=(c == C - 1):
              e.matmul(out=o, lhsT=l, rhs=r, start=st, stop=sp), [xt, g], [out])
    la = pa[:, 0:128]
    lap3 = la.r(lambda ap: ap.rearrange("p (a b) -> p a b", b=8))
    mx = P.fstat[:, 0:16]
    E("V", lambda e, o=mx.ap, i=lap3.ap: e.tensor_reduce(out=o, in_=i, axis=AX.X, op=ALU.max), [la], [mx])
    d1 = P.carve(67584, 128, 128, F32)[:, 0:128]
    d13 = d1.r(lambda ap: ap.rearrange("p (a b) -> p a b", b=8))
    d2 = P.carve(68608, 128, 128, F32)[:, 0:128]
    d23 = d2.r(lambda ap: ap.rearrange("p (a b) -> p a b", b=8))
    E("V", lambda e, o=d13.ap, i=lap3.ap, m=mx.ap.unsqueeze(-1).broadcast_to((128, 16, 8)):
      e.tensor_tensor(out=o, in0=i, in1=m, op=ALU.subtract), [la, mx], [d1])
    E("S", lambda e, o=d2.ap, i=d1.ap: e.activation(out=o, in_=i, func=AF.Exp), [d1], [d2])
    sm = P.fstat[:, 16:32]
    E("V", lambda e, o=sm.ap, i=d23.ap: e.tensor_reduce(out=o, in_=i, axis=AX.X, op=ALU.add), [d2], [sm])
    sr = P.fstat[:, 64:80]
    E("V", lambda e, o=sr.ap, i=sm.ap: e.reciprocal(out=o, in_=i), [sm], [sr], force=True)
    pr = P.probs[:, 0:128]
    pr3 = pr.r(lambda ap: ap.rearrange("p (a b) -> p a b", b=8))
    E("V", lambda e, o=pr3.ap, i=d23.ap, m=sr.ap.unsqueeze(-1).broadcast_to((128, 16, 8)):
      e.tensor_tensor(out=o, in0=i, in1=m, op=ALU.mult), [d2, sr], [pr], force=True)
    sel = P.fscr[:, 0:128]
    sel3 = sel.r(lambda ap: ap.rearrange("p (a b) -> p a b", b=8))
    if sel_biased:
        E("V", lambda e, o=sel3.ap, i=pr3.ap, m=gbc[:, 0:8].ap.unsqueeze(1).broadcast_to((128, 16, 8)):
          e.tensor_tensor(out=o, in0=i, in1=m, op=ALU.add), [pr, gbc[:, 0:8]], [sel])
        base = sel
    else:
        E("V", lambda e, o=sel.ap, i=pr.ap: e.tensor_copy(out=o, in_=i), [pr], [sel])
        base = sel
    m1 = P.fstat[:, 32:48]
    E("V", lambda e, o=m1.ap, i=sel3.ap: e.tensor_reduce(out=o, in_=i, axis=AX.X, op=ALU.max), [sel], [m1])
    eqm = P.tmpf[:, 0:128]
    eqm3 = eqm.r(lambda ap: ap.rearrange("p (a b) -> p a b", b=8))
    E("V", lambda e, o=eqm3.ap, i=sel3.ap, m=m1.ap.unsqueeze(-1).broadcast_to((128, 16, 8)):
      e.tensor_tensor(out=o, in0=i, in1=m, op=ALU.is_ge), [sel, m1], [eqm])
    E("V", lambda e, o=eqm3.ap, i=eqm3.ap, s=sel3.ap:
      e.scalar_tensor_tensor(out=o, in0=i, scalar=NEG, in1=s, op0=ALU.mult, op1=ALU.add), [eqm, sel], [eqm])
    m2 = P.fstat[:, 48:64]
    E("V", lambda e, o=m2.ap, i=eqm3.ap: e.tensor_reduce(out=o, in_=i, axis=AX.X, op=ALU.max), [eqm], [m2])
    E("V", lambda e, o=eqm3.ap, i=sel3.ap, m=m2.ap.unsqueeze(-1).broadcast_to((128, 16, 8)):
      e.tensor_tensor(out=o, in0=i, in1=m, op=ALU.is_ge), [sel, m2], [eqm])
    ctm = coef_tm[:, 0:128]
    ctm3 = ctm.r(lambda ap: ap.rearrange("p (a b) -> p a b", b=8))
    E("V", lambda e, o=ctm3.ap, i=pr3.ap, m=eqm3.ap: e.tensor_tensor(out=o, in0=i, in1=m, op=ALU.mult),
      [pr, eqm], [ctm])
    for tc in range(16):
        pt = P.bank(5, BF16)[0:8, 0:128]
        inp = coef_tm[:, tc * 8:(tc + 1) * 8]
        E("T", lambda e, o=pt.ap, i=inp.ap, idm=P.idbf[:, :].ap: e.transpose(out=o, in_=i, identity=idm),
          [inp, P.idbf[:, :]], [pt])
        dstc = ctile[0:8, tc * 128:(tc + 1) * 128]
        E("V", lambda e, o=dstc.ap, i=pt.ap: e.tensor_copy(out=o, in_=i), [pt], [dstc])

    for e_i in (range(-1, nE) if experts is None else experts):
        routed = e_i >= 0
        oc = OC if routed else OCS
        dloc = dh if routed else dhs
        dkc = dloc // 128
        if routed:
            if has_bias:
                P.dma("G", P.rowA[0:1, 0:dloc], APK(P.din(b_names[0], [nE, dloc])[e_i:e_i + 1, :], []))
                P.dma("G", P.rowB[0:1, 0:dloc], APK(P.din(b_names[1], [nE, dloc])[e_i:e_i + 1, :], []))
                P.dma("G", P.rowC[0:1, 0:1024], APK(P.din(b_names[2], [nE, 1024])[e_i:e_i + 1, :], []))
            for wv, nm in ((w1v, w_names[0]), (w3v, w_names[1])):
                src = P.din(nm, [nE, DM, dloc])
                for c in range(C):
                    P.dma("G", wv[:, c * dloc:(c + 1) * dloc],
                          APK(src[e_i, c * 128:(c + 1) * 128, :], []))
            src = P.din(w_names[2], [nE, dloc, DM])
            for c in range(dkc):
                P.dma("G", w2v[:, c * 1024:(c + 1) * 1024],
                      APK(src[e_i, c * 128:(c + 1) * 128, :], []))
            P.dma("QS", coefrow[0:1, 0:2048], ctile[e_i:e_i + 1, 0:2048])
        else:
            if s_bnames:
                P.dma("G", P.rowA[0:1, 0:dloc], APK(P.din(s_bnames[0], [dloc]).unsqueeze(0), []))
                P.dma("G", P.rowB[0:1, 0:dloc], APK(P.din(s_bnames[1], [dloc]).unsqueeze(0), []))
                P.dma("G", P.rowC[0:1, 0:1024], APK(P.din(s_bnames[2], [DM]).unsqueeze(0), []))
            for wv, nm in ((w1v, s_names[0]), (w3v, s_names[1])):
                src = P.din(nm, [DM, dloc])
                for c in range(C):
                    P.dma("G", wv[:, c * dloc:(c + 1) * dloc],
                          APK(src[c * 128:(c + 1) * 128, :], []))
            src = P.din(s_names[2], [dloc, DM])
            for c in range(dkc):
                P.dma("G", w2v[:, c * 1024:(c + 1) * 1024],
                      APK(src[c * 128:(c + 1) * 128, :], []))

        use_bias = has_bias

        for tt in range(NTT):
            t0, t1 = tt * TT, (tt + 1) * TT
            if routed:
                pd = P.bank(6)[:, 0:TT]
                cr = coefrow[0:1, t0:t1]
                E("T", lambda e, o=pd.ap, l=P.ones_r[0:1, 0:128].ap, r=cr.ap:
                  e.matmul(out=o, lhsT=l, rhs=r, start=True, stop=True), [cr, P.ones_r[0:1, 0:128]], [pd])
                E("V", lambda e, o=cbc[:, 0:TT].ap, i=pd.ap:
                  e.tensor_single_scalar(out=o, in_=i, scalar=0.5, op=ALU.mult), [pd], [cbc[:, 0:TT]])
            for o_i in range(oc):
                pA = P.bank((o_i % 3) * 2)[:, 0:TT]
                pB = P.bank((o_i % 3) * 2 + 1)[:, 0:TT]
                if use_bias and tt == 0:
                    emit_bcol(P, P.rowA, o_i, P.bcA)
                    emit_bcol(P, P.rowB, o_i, P.bcB)
                for (pX, wv_) in ((pA, w1v), (pB, w3v)):
                    for c in range(C):
                        lt = wv_[:, c * dloc + o_i * 128: c * dloc + (o_i + 1) * 128]
                        rt = P.fm(P.xh, c, t0, t1)
                        E("T", lambda e, o=pX.ap, l=lt.ap, r=rt.ap, st=(c == 0), sp=(c == C - 1):
                          e.matmul(out=o, lhsT=l, rhs=r, start=st, stop=sp), [lt, rt], [pX])
                ht = hv[:, o_i * TT:(o_i + 1) * TT]
                if silu_prod:
                    tf = (P.tmpf if o_i % 2 == 0 else P.tmpf2)[:, :]
                    ba = P.bcA[:, o_i:o_i + 1]
                    bb = P.bcB[:, o_i:o_i + 1]
                    E("S", lambda e, o=tf.ap, i=pA.ap, b=ba.ap:
                      e.activation(out=o, in_=i, func=AF.Identity, bias=b), [pA, ba], [tf])
                    tg = (P.tmp1 if o_i % 2 == 0 else P.tmp2)[:, :]
                    E("V", lambda e, o=tg.ap, i=pB.ap, b=bb.ap, j=tf.ap:
                      e.scalar_tensor_tensor(out=o, in0=i, scalar=b, in1=j, op0=ALU.add, op1=ALU.mult),
                      [pB, bb, tf], [tg])
                    E("S", lambda e, o=ht.ap, i=tg.ap: e.activation(out=o, in_=i, func=AF.Silu), [tg], [ht])
                else:
                    t1b = (P.tmp1 if o_i % 2 == 0 else P.tmp2)[:, :]
                    E("S", lambda e, o=t1b.ap, i=pA.ap: e.activation(out=o, in_=i, func=AF.Silu), [pA], [t1b])
                    E("V", lambda e, o=ht.ap, i=pB.ap, j=t1b.ap: e.tensor_tensor(out=o, in0=i, in1=j, op=ALU.mult),
                      [pB, t1b], [ht])
            for d_i in range(C):
                pC = P.bank(6 + d_i % 2)[:, 0:TT]
                if use_bias and tt == 0:
                    emit_bcol(P, P.rowC, d_i, P.bcC)
                for k in range(dkc):
                    lt = w2v[:, k * 1024 + d_i * 128: k * 1024 + (d_i + 1) * 128]
                    rt = hv[:, k * TT:(k + 1) * TT]
                    E("T", lambda e, o=pC.ap, l=lt.ap, r=rt.ap, st=(k == 0), sp=(k == dkc - 1):
                      e.matmul(out=o, lhsT=l, rhs=r, start=st, stop=sp), [lt, rt], [pC])
                xrt = P.fm(P.xr, d_i, t0, t1)
                bc_ = P.bcC[:, d_i:d_i + 1]
                if routed:
                    tf = (P.tmpf if d_i % 2 == 0 else P.tmpf2)[:, :]
                    if use_bias:
                        E("V", lambda e, o=tf.ap, i=pC.ap, b=bc_.ap, j=cbc[:, 0:TT].ap:
                          e.scalar_tensor_tensor(out=o, in0=i, scalar=b, in1=j, op0=ALU.add, op1=ALU.mult),
                          [pC, bc_, cbc[:, 0:TT]], [tf])
                    else:
                        E("V", lambda e, o=tf.ap, i=pC.ap, j=cbc[:, 0:TT].ap:
                          e.tensor_tensor(out=o, in0=i, in1=j, op=ALU.mult), [pC, cbc[:, 0:TT]], [tf])
                    E("G", lambda e, o=xrt.ap, i=tf.ap, x=xrt.ap:
                      e.tensor_tensor(out=o, in0=i, in1=x, op=ALU.add), [tf, xrt], [xrt])
                elif use_bias:
                    tf = (P.tmpf if d_i % 2 == 0 else P.tmpf2)[:, :]
                    E("V", lambda e, o=tf.ap, i=pC.ap, b=bc_.ap:
                      e.tensor_scalar(out=o, in0=i, scalar1=b, scalar2=0.5, op0=ALU.add, op1=ALU.mult),
                      [pC, bc_], [tf])
                    E("G", lambda e, o=xrt.ap, i=tf.ap, x=xrt.ap:
                      e.tensor_tensor(out=o, in0=i, in1=x, op=ALU.add), [tf, xrt], [xrt])
                else:
                    E("V", lambda e, o=xrt.ap, i=pC.ap, x=xrt.ap:
                      e.scalar_tensor_tensor(out=o, in0=i, scalar=0.5, in1=x, op0=ALU.mult, op1=ALU.add),
                      [pC, xrt], [xrt])



def _make_consts():
    idm = np.eye(128, dtype=ml_dtypes.bfloat16)
    tri = np.triu(np.ones((128, 128), dtype=np.float32)).astype(ml_dtypes.bfloat16)
    quad = np.full((128, 8 * 128), NEG, dtype=np.float32)
    q = np.zeros((128, 128), dtype=np.float32)
    q[0:64, 0:64] = 1
    q[64:128, 64:128] = 1
    for j in range(8):
        quad[:, j * 128:(j + 1) * 128] = np.where(q > 0, 0.0, NEG)
    return {
        "c_id": idm,
        "c_tri": tri,
        "c_quad": quad.astype(ml_dtypes.bfloat16),
        "c_ones_r": np.ones((1, 512), dtype=ml_dtypes.bfloat16),
        "c_ones_c": np.ones((128, 1), dtype=ml_dtypes.bfloat16),
        "c_ones_f": np.ones((1, 128), dtype=np.float32),
        "c_eps": np.array([[1e-5, 1e-6, 0, 0, 0, 0, 0, 0]], dtype=np.float32),
    }


PHASE_EMIT = {}


def _register_phases():
    PHASE_EMIT.update({
        "ffn0": lambda P: (emit_ln(P), emit_moe_ffn(P, 0)),
        "ffn1": lambda P: (emit_ln(P), emit_moe_ffn(P, 1)),
        "attn0": lambda P: (emit_ln(P), emit_attn0(P)),
        "attn1": lambda P: (emit_ln(P), emit_attn1(P)),
    })


def build_program(phases, debug=()):
    P = Prog(phases)
    emit_consts(P)
    emit_load_x(P)
    for ph in phases:
        PHASE_EMIT[ph](P)
    emit_store_x(P)
    for i, (nm, tv, p0, p1, c0, c1) in enumerate(debug):
        dt = BF16 if tv(P).esz == 2 else F32
        dd = P.nc.dram_tensor("dbg_" + nm, [p1 - p0, c1 - c0], dt, kind="ExternalOutput").ap()
        reg = tv(P)[p0:p1, c0:c1]
        P.em.emit("QS", lambda e, d=dd, s=reg.ap: e.dma_start(out=d, in_=s), [reg], ["dram:dbg"], kind="d")
    if debug:
        P.em.emit("QS", None, R=["dram:dbg"], W=())
    nc = P.finalize()
    return P, nc


ALL_PHASES = ["ffn0", "attn0", "ffn0", "ffn1", "attn1", "ffn1"]


def kernel(**inputs):
    _register_phases()
    consts = _make_consts()
    x = np.asarray(inputs["L0_attn_x"], dtype=np.float32)
    n = 8
    base = {k: np.asarray(v, dtype=np.float32) for k, v in inputs.items() if k != "L0_attn_x"}

    xs = [np.ascontiguousarray(x[i].T).reshape(C, 128, S) for i in range(n)]

    import os
    if os.environ.get("MK_UNFUSED"):
        groups = [[ph] for ph in ALL_PHASES]
    else:
        groups = [ALL_PHASES]
    cur = xs
    for g in groups:
        P, nc = build_program(g)
        in_maps = []
        for i in range(n):
            m = {"x": cur[i]}
            for nm in P.in_names:
                if nm == "x":
                    continue
                if nm.startswith("c_"):
                    m[nm] = consts[nm]
                else:
                    m[nm] = base[nm]
            in_maps.append(m)
        res = run_bass_kernel_spmd(nc, in_maps, core_ids=list(range(n)))
        cur = [res.results[i]["y"] for i in range(n)]

    out = np.stack([cur[i].reshape(DM, S).T for i in range(n)]).astype(np.float32)
    return out





def _attn_carves(P):
    c = {}
    c["K"] = P.carve(0, 128, 8 * 1024, BF16)
    c["Q"] = P.carve(16384, 128, 8 * 1024, BF16)
    c["V"] = P.carve(32768, 128, 8 * 1024, BF16)
    c["W"] = P.carve(49152, 128, 8 * 1024, BF16)
    c["pT"] = P.carve(65536, 128, 1024, BF16)
    c["pTg"] = P.carve(67584, 8, 1024, BF16)
    c["ssb"] = P.carve(69632, 128, 1024, BF16)
    c["png"] = P.carve(71680, 128, 128, BF16)
    c["kg"] = P.carve(71936, 128, 64, BF16)
    c["kg2"] = P.carve(84576, 128, 128, BF16)
    c["Qz"] = P.carve(84832, 128, 2048, BF16)
    c["qg"] = P.carve(72064, 128, 64, BF16)
    c["xhg"] = P.carve(72192, 128, 64, BF16)
    c["vg"] = P.carve(72320, 8, 1024, BF16)
    c["gp"] = P.carve(74368, 128, 1024, BF16)
    c["gst"] = P.carve(76416, 128, 40, F32)
    c["og0"] = P.carve(76576, 64, 128, F32)
    c["og1"] = P.carve(77088, 64, 128, F32)
    c["ogf"] = P.carve(77600, 64, 128, BF16)
    c["cb0"] = P.carve(77856, 64, 128, BF16)
    c["cb1"] = P.carve(78112, 64, 128, BF16)
    c["crow"] = P.carve(78368, 1, 256, BF16)
    c["KVs"] = P.carve(78880, 128, 16 * 65, F32)
    c["am"] = P.carve(83040, 128, 512, BF16)
    c["ktm"] = P.carve(84064, 128, 256, BF16)
    c["KVb"] = P.carve(84832, 128, 16 * 65, BF16)
    c["rr"] = P.carve(74368, 1, 512, F32)
    c["rb"] = P.carve(76416, 64, 512, F32)
    return c


def _proj_fm(P, W, dst, brow, t0, t1, scale=None, phi=False, bcol=None, build_cols=False):
    E = P.E
    h0 = (t0 // HS) * HS
    for oc in range(C):
        if build_cols:
            emit_bcol(P, brow, oc, bcol, scale=scale)
        bc_ = bcol[:, oc:oc + 1]
        for tt0 in range(t0, t1, TT):
            pX = P.bank(oc % 4)[:, 0:TT]
            for ch in range(C):
                lt = W[:, ch * 1024 + oc * 128: ch * 1024 + (oc + 1) * 128]
                rt = P.fm(P.xh, ch, tt0, tt0 + TT)
                E("T", lambda e, o=pX.ap, l=lt.ap, r=rt.ap, st=(ch == 0), sp=(ch == C - 1):
                  e.matmul(out=o, lhsT=l, rhs=r, start=st, stop=sp), [lt, rt], [pX])
            dcol = oc * 1024 + (tt0 - h0)
            dt_ = dst[:, dcol:dcol + TT]
            if phi:
                a = P.tmpf[:, :]
                b2 = P.tmpf2[:, :]
                E("V", lambda e, o=a.ap, i=pX.ap, b=bc_.ap:
                  e.tensor_scalar(out=o, in0=i, scalar1=b, scalar2=0.0, op0=ALU.add, op1=ALU.min),
                  [pX, bc_], [a])
                E("S", lambda e, o=b2.ap, i=a.ap: e.activation(out=o, in_=i, func=AF.Exp), [a], [b2])
                E("V", lambda e, o=a.ap, i=pX.ap, b=bc_.ap:
                  e.tensor_scalar(out=o, in0=i, scalar1=b, scalar2=0.0, op0=ALU.add, op1=ALU.max),
                  [pX, bc_], [a])
                E("V", lambda e, o=dt_.ap, i=b2.ap, j=a.ap: e.tensor_tensor(out=o, in0=i, in1=j, op=ALU.add),
                  [a, b2], [dt_])
            elif scale is not None:
                E("S", lambda e, o=dt_.ap, i=pX.ap, sc=scale, b=bc_.ap:
                  e.activation(out=o, in_=i, func=AF.Identity, scale=sc, bias=b), [pX, bc_], [dt_])
            else:
                E("V", lambda e, o=dt_.ap, i=pX.ap, b=bc_.ap:
                  e.tensor_scalar(out=o, in0=i, scalar1=b, scalar2=None, op0=ALU.add, op1=ALU.bypass),
                  [pX, bc_], [dt_])


def _proj_tm(P, W, dst, brow, t0, t1):
    E = P.E
    h0 = (t0 // HS) * HS
    for tk0 in range(t0, t1, 128):
        for n in range(2):
            pX = P.bank(4 + ((tk0 // 128) % 2))[:, 0:TT]
            br = brow[0:1, n * 512:(n + 1) * 512]
            E("T", lambda e, o=pX.ap, l=P.ones_r[0:1, 0:128].ap, r=br.ap:
              e.matmul(out=o, lhsT=l, rhs=r, start=True, stop=False), [br, P.ones_r[0:1, 0:128]], [pX])
            for ch in range(C):
                lt = P.fm(P.xh, ch, tk0, tk0 + 128)
                rt = W[:, ch * 1024 + n * 512: ch * 1024 + (n + 1) * 512]
                E("T", lambda e, o=pX.ap, l=lt.ap, r=rt.ap, sp=(ch == C - 1):
                  e.matmul(out=o, lhsT=l, rhs=r, start=False, stop=sp), [lt, rt], [pX])
            dcol = ((tk0 - h0) // 128) * 1024 + n * 512
            dt_ = dst[:, dcol:dcol + TT]
            E("V", lambda e, o=dt_.ap, i=pX.ap: e.tensor_copy(out=o, in_=i), [pX], [dt_])


def _proj_small(P, W, bcol, dst, scale, xsrc):
    E = P.E
    for oc in range(C):
        pX = P.bank(3)[:, 0:8]
        bc_ = bcol[:, oc:oc + 1]
        for ch in range(C):
            lt = W[:, ch * 1024 + oc * 128: ch * 1024 + (oc + 1) * 128]
            rt = xsrc[:, ch * 8:(ch + 1) * 8]
            E("T", lambda e, o=pX.ap, l=lt.ap, r=rt.ap, st=(ch == 0), sp=(ch == C - 1):
              e.matmul(out=o, lhsT=l, rhs=r, start=st, stop=sp), [lt, rt], [pX])
        d_ = dst[:, oc * 8:(oc + 1) * 8]
        if scale is not None:
            E("S", lambda e, o=d_.ap, i=pX.ap, sc=scale, b=bc_.ap:
              e.activation(out=o, in_=i, func=AF.Identity, scale=sc, bias=b), [pX, bc_], [d_])
        else:
            E("V", lambda e, o=d_.ap, i=pX.ap, b=bc_.ap:
              e.tensor_scalar(out=o, in0=i, scalar1=b, scalar2=None, op0=ALU.add, op1=ALU.bypass),
              [pX, bc_], [d_])


def _load_wmat(P, cv, name):
    src = P.din(name, [DM, DM])
    for ch in range(C):
        P.dma("G", cv["W"][:, ch * 1024:(ch + 1) * 1024], APK(src[ch * 128:(ch + 1) * 128, :], []))


def _outproj(P, cv, wname, bname):
    E = P.E
    _load_wmat(P, cv, wname)
    P.dma("G", P.rowA[0:1, 0:1024], APK(P.din(bname, [DM]).unsqueeze(0), []))
    W_ = cv["W"]
    for oc in range(C):
        emit_bcol(P, P.rowA, oc, P.bcA)
        bc_ = P.bcA[:, oc:oc + 1]
        for tt in range(NTT):
            pX = P.bank(oc % 4)[:, 0:TT]
            for ch in range(C):
                lt = W_[:, ch * 1024 + oc * 128: ch * 1024 + (oc + 1) * 128]
                rt = P.fm(P.xh, ch, tt * TT, (tt + 1) * TT)
                E("T", lambda e, o=pX.ap, l=lt.ap, r=rt.ap, st=(ch == 0), sp=(ch == C - 1):
                  e.matmul(out=o, lhsT=l, rhs=r, start=st, stop=sp), [lt, rt], [pX])
            xrt = P.fm(P.xr, oc, tt * TT, (tt + 1) * TT)
            E("V", lambda e, o=xrt.ap, i=pX.ap, b=bc_.ap, x=xrt.ap:
              e.scalar_tensor_tensor(out=o, in0=i, scalar=b, in1=x, op0=ALU.add, op1=ALU.add),
              [pX, bc_, xrt], [xrt])


ATTN0_PARTS = {"qkv", "local", "glob", "comb", "outproj"}
A0LD = 9
A0NB = 8
A0NH = 8
A0NG = 2


def emit_attn0(P):
    E = P.E
    cv = _attn_carves(P)
    K_, Q_, V_, W_ = cv["K"], cv["Q"], cv["V"], cv["W"]
    ssb, png, pT, pTg = cv["ssb"], cv["png"], cv["pT"], cv["pTg"]
    P.dma("G", P.rowA[0:1, 0:1024], APK(P.din("L0_attn_bq", [DM]).unsqueeze(0), []))
    P.dma("G", P.rowB[0:1, 0:1024], APK(P.din("L0_attn_bk", [DM]).unsqueeze(0), []))
    P.dma("G", P.rowC[0:1, 0:1024], APK(P.din("L0_attn_bv", [DM]).unsqueeze(0), []))
    for ch in range(C):
        for j, tg in enumerate(GLOBAL_TOKENS):
            s_ = P.fm(P.xh, ch, tg, tg + 1)
            d_ = cv["xhg"][:, ch * 8 + j:ch * 8 + j + 1]
            E("V", lambda e, o=d_.ap, i=s_.ap: e.tensor_copy(out=o, in_=i), [s_], [d_])

    SC = 1.0 / 8.0
    for half in range(2):
        t0, t1 = half * HS, (half + 1) * HS
        _load_wmat(P, cv, "L0_attn_Wq")
        _proj_fm(P, W_, Q_, P.rowA, t0, t1, scale=SC, bcol=P.bcA, build_cols=(half == 0))
        if half == 0:
            _proj_small(P, W_, P.bcA, cv["qg"], SC, cv["xhg"])
        _load_wmat(P, cv, "L0_attn_Wk")
        _proj_fm(P, W_, K_, P.rowB, t0, t1, bcol=P.bcB, build_cols=(half == 0))
        if half == 0:
            _proj_small(P, W_, P.bcB, cv["kg"], None, cv["xhg"])
        _load_wmat(P, cv, "L0_attn_Wv")
        _proj_tm(P, W_, V_, P.rowC, t0, t1)
        if half == 0:
            for n in range(2):
                pX = P.bank(3)[0:8, 0:TT]
                E("T", lambda e, o=pX.ap, l=P.ones_r[0:1, 0:8].ap, r=P.rowC[0:1, n * 512:(n + 1) * 512].ap:
                  e.matmul(out=o, lhsT=l, rhs=r, start=True, stop=False),
                  [P.rowC[0:1, n * 512:(n + 1) * 512], P.ones_r[0:1, 0:8]], [pX])
                for ch in range(C):
                    lt = cv["xhg"][:, ch * 8:(ch + 1) * 8]
                    rt = W_[:, ch * 1024 + n * 512: ch * 1024 + (n + 1) * 512]
                    E("T", lambda e, o=pX.ap, l=lt.ap, r=rt.ap, sp=(ch == C - 1):
                      e.matmul(out=o, lhsT=l, rhs=r, start=False, stop=sp), [lt, rt], [pX])
                d_ = cv["vg"][0:8, n * 512:(n + 1) * 512]
                E("V", lambda e, o=d_.ap, i=pX.ap: e.tensor_copy(out=o, in_=i), [pX], [d_])

        if half == 0:
            kg2 = cv["kg2"]
            E("V", lambda e, o=kg2[0:128, 0:128].ap: e.memset(o, 0.0), [], [kg2[0:128, 0:128]])
            E("V", lambda e, o=cv["Qz"][0:128, 0:2048].ap: e.memset(o, 0.0), [], [cv["Qz"][0:128, 0:2048]])
            for h in range(NH):
                prow, chh = (h % 2) * 64, h // 2
                s_ = cv["kg"][prow:prow + 64, chh * 8:(chh + 1) * 8]
                d_ = kg2[prow:prow + 64, h * 8:(h + 1) * 8]
                E("V", lambda e, o=d_.ap, i=s_.ap: e.tensor_copy(out=o, in_=i), [s_], [d_])

        def a0_scores(bl, grp):
            if grp == 0:
                for h in range(NH):
                    prow, chh = (h % 2) * 64, h // 2
                    s_ = Q_[prow:prow + 64, chh * 1024 + bl * 128: chh * 1024 + (bl + 1) * 128]
                    d_ = cv["Qz"][prow:prow + 64, h * 128:(h + 1) * 128]
                    E("V", lambda e, o=d_.ap, i=s_.ap: e.tensor_copy(out=o, in_=i), [s_], [d_])
            pl = P.banks(grp * 2, 2)
            for hh in range(8):
                h = grp * 8 + hh
                chh = h // 2
                qsg = Q_[0:128, chh * 1024 + bl * 128: chh * 1024 + (bl + 1) * 128]
                kgs = cv["kg2"][0:128, h * 8:(h + 1) * 8]
                og_ = P.bank(4)[:, h * 8:(h + 1) * 8]
                E("T", lambda e, o=og_.ap, l=qsg.ap, r=kgs.ap:
                  e.matmul(out=o, lhsT=l, rhs=r, start=True, stop=True), [qsg, kgs], [og_])
                qs = cv["Qz"][0:128, h * 128:(h + 1) * 128]
                ks = K_[0:128, chh * 1024 + bl * 128: chh * 1024 + (bl + 1) * 128]
                ol = pl[:, hh * 128:(hh + 1) * 128]
                E("T", lambda e, o=ol.ap, l=qs.ap, r=ks.ap:
                  e.matmul(out=o, lhsT=l, rhs=r, start=True, stop=True), [qs, ks], [ol])

        def a0_softmax(bl, grp):
            b = half * 8 + bl
            gin = [g for g in GLOBAL_TOKENS if b * 128 <= g < (b + 1) * 128]
            hbase = grp * 8
            pl = P.banks(grp * 2, 2)
            ssb = cv["ssb"] if grp == 0 else cv["gp"]
            fo = grp * 40
            E("V", lambda e, o=ssb[:, 0:1024].ap, i=pl[:, 0:1024].ap, m=P.quad[:, 0:1024].ap:
              e.tensor_tensor(out=o, in0=i, in1=m, op=ALU.add),
              [pl[:, 0:1024], P.quad[:, 0:1024]], [ssb[:, 0:1024]])
            for g in gin:
                off = g - b * 128
                for hh in range(8):
                    s_ = pl[:, hh * 128 + off: hh * 128 + off + 1]
                    d_ = ssb[:, hh * 128 + off: hh * 128 + off + 1]
                    E("V", lambda e, o=d_.ap, i=s_.ap: e.tensor_copy(out=o, in_=i), [s_], [d_])
            s3 = ssb[:, 0:1024].r(lambda ap: ap.rearrange("p (a b) -> p a b", b=128))
            gg = P.bank(4)[:, hbase * 8:(hbase + 8) * 8]
            gg3 = gg.r(lambda ap: ap.rearrange("p (a b) -> p a b", b=8))
            mx1 = P.fstat[:, fo + 0:fo + 8]
            E("V", lambda e, o=mx1.ap, i=s3.ap: e.tensor_reduce(out=o, in_=i, axis=AX.X, op=ALU.max),
              [ssb[:, 0:1024]], [mx1])
            mx2 = P.fstat[:, fo + 8:fo + 16]
            E("V", lambda e, o=mx2.ap, i=gg3.ap: e.tensor_reduce(out=o, in_=i, axis=AX.X, op=ALU.max),
              [gg], [mx2])
            E("V", lambda e, o=mx1.ap, i=mx1.ap, j=mx2.ap: e.tensor_tensor(out=o, in0=i, in1=j, op=ALU.max),
              [mx1, mx2], [mx1])
            E("V", lambda e, o=s3.ap, i=s3.ap, m=mx1.ap.unsqueeze(-1).broadcast_to((128, 8, 128)):
              e.tensor_tensor(out=o, in0=i, in1=m, op=ALU.subtract), [ssb[:, 0:1024], mx1], [ssb[:, 0:1024]])
            pn_g = png[:, hbase * 8:(hbase + 8) * 8]
            pn_g3 = pn_g.r(lambda ap: ap.rearrange("p (a b) -> p a b", b=8))
            E("V", lambda e, o=pn_g3.ap, i=gg3.ap, m=mx1.ap.unsqueeze(-1).broadcast_to((128, 8, 8)):
              e.tensor_tensor(out=o, in0=i, in1=m, op=ALU.subtract), [gg, mx1], [pn_g])
            E("S", lambda e, o=ssb[:, 0:1024].ap, i=ssb[:, 0:1024].ap:
              e.activation(out=o, in_=i, func=AF.Exp), [ssb[:, 0:1024]], [ssb[:, 0:1024]])
            E("S", lambda e, o=pn_g.ap, i=pn_g.ap: e.activation(out=o, in_=i, func=AF.Exp),
              [pn_g], [pn_g])
            for g in gin:
                gi = GLOBAL_TOKENS.index(g)
                for hh in range(8):
                    d_ = png[:, (hbase + hh) * 8 + gi:(hbase + hh) * 8 + gi + 1]
                    E("V", lambda e, o=d_.ap: e.memset(o, 0.0), [], [d_])
            s1 = P.fstat[:, fo + 16:fo + 24]
            E("V", lambda e, o=s1.ap, i=s3.ap: e.tensor_reduce(out=o, in_=i, axis=AX.X, op=ALU.add),
              [ssb[:, 0:1024]], [s1])
            s2 = P.fstat[:, fo + 24:fo + 32]
            E("V", lambda e, o=s2.ap, i=pn_g3.ap: e.tensor_reduce(out=o, in_=i, axis=AX.X, op=ALU.add),
              [pn_g], [s2])
            E("V", lambda e, o=s1.ap, i=s1.ap, j=s2.ap: e.tensor_tensor(out=o, in0=i, in1=j, op=ALU.add),
              [s1, s2], [s1])
            rc = P.fstat[:, fo + 32:fo + 40]
            E("V", lambda e, o=rc.ap, i=s1.ap: e.reciprocal(out=o, in_=i), [s1], [rc])
            E("V", lambda e, o=s3.ap, i=s3.ap, m=rc.ap.unsqueeze(-1).broadcast_to((128, 8, 128)):
              e.tensor_tensor(out=o, in0=i, in1=m, op=ALU.mult), [ssb[:, 0:1024], rc], [ssb[:, 0:1024]])
            E("V", lambda e, o=pn_g3.ap, i=pn_g3.ap, m=rc.ap.unsqueeze(-1).broadcast_to((128, 8, 8)):
              e.tensor_tensor(out=o, in0=i, in1=m, op=ALU.mult), [pn_g, rc], [pn_g])

        def a0_tav(bl, grp):
            hbase = grp * 8
            ssb = cv["ssb"] if grp == 0 else cv["gp"]
            for hh in range(8):
                h = hbase + hh
                prow, chh = (h % 2) * 64, h // 2
                ps = ssb[:, hh * 128:(hh + 1) * 128]
                ptp = P.bank(5, BF16)[0:128, (hh % 4) * 128:((hh % 4) + 1) * 128]
                E("T", lambda e, o=ptp.ap, i=ps.ap, idm=P.idbf[:, :].ap:
                  e.transpose(out=o, in_=i, identity=idm), [ps, P.idbf[:, :]], [ptp])
                d_ = pT[:, hh * 128:(hh + 1) * 128]
                E("V", lambda e, o=d_.ap, i=ptp.ap: e.tensor_copy(out=o, in_=i), [ptp], [d_])
                pgs = png[:, h * 8:(h + 1) * 8]
                ptg = P.bank(5, BF16)[0:8, 512 + (hh % 4) * 128: 512 + ((hh % 4) + 1) * 128]
                E("T", lambda e, o=ptg.ap, i=pgs.ap, idm=P.idbf[:, :].ap:
                  e.transpose(out=o, in_=i, identity=idm), [pgs, P.idbf[:, :]], [ptg])
                dg_ = pTg[0:8, hh * 128:(hh + 1) * 128]
                E("V", lambda e, o=dg_.ap, i=ptg.ap: e.tensor_copy(out=o, in_=i), [ptg], [dg_])
                po = P.bank(6 + grp)[0:64, (hh % 4) * 128:((hh % 4) + 1) * 128]
                vloc = V_[:, bl * 1024 + h * 64: bl * 1024 + (h + 1) * 64]
                E("T", lambda e, o=po.ap, l=vloc.ap, r=d_.ap:
                  e.matmul(out=o, lhsT=l, rhs=r, start=True, stop=False), [vloc, d_], [po])
                vgs = cv["vg"][0:8, h * 64:(h + 1) * 64]
                E("T", lambda e, o=po.ap, l=vgs.ap, r=dg_.ap:
                  e.matmul(out=o, lhsT=l, rhs=r, start=False, stop=True), [vgs, dg_], [po])
                ao = P.xh[prow:prow + 64,
                          chh * S + half * HS + bl * 128: chh * S + half * HS + (bl + 1) * 128]
                E("V", lambda e, o=ao.ap, i=po.ap: e.tensor_copy(out=o, in_=i), [po], [ao])

        if "local" in ATTN0_PARTS:
            prev = None
            for bl in range(8):
                for grp in range(2):
                    a0_scores(bl, grp)
                    a0_softmax(bl, grp)
                    if prev is not None:
                        a0_tav(*prev)
                    prev = (bl, grp)
            a0_tav(*prev)

        og = cv["og0"] if half == 0 else cv["og1"]
        for grp in (range(6) if "glob" in ATTN0_PARTS else []):
            heads = [3 * grp + j for j in range(3) if 3 * grp + j < NH]
            psc = P.banks(0, 2)
            for j, h in enumerate(heads):
                prow, chh = (h % 2) * 64, h // 2
                qgs = cv["qg"][prow:prow + 64, chh * 8:(chh + 1) * 8]
                for n in range(2):
                    ksl = K_[prow:prow + 64, chh * 1024 + n * 512: chh * 1024 + (n + 1) * 512]
                    o_ = psc[32 * j:32 * j + 8, n * 512:(n + 1) * 512]
                    E("T", lambda e, o=o_.ap, l=qgs.ap, r=ksl.ap:
                      e.matmul(out=o, lhsT=l, rhs=r, start=True, stop=True), [qgs, ksl], [o_])
            gm = P.fstat[:, 40:41]
            E("V", lambda e, o=gm.ap, i=psc[:, 0:1024].ap:
              e.tensor_reduce(out=o, in_=i, axis=AX.X, op=ALU.max), [psc[:, 0:1024]], [gm])
            ngm = P.fstat[:, 41:42]
            E("V", lambda e, o=ngm.ap, i=gm.ap: e.tensor_single_scalar(out=o, in_=i, scalar=-1.0, op=ALU.mult),
              [gm], [ngm])
            gpv = cv["gp"]
            gpb = gpv[:, 0:1024]
            E("S", lambda e, o=gpb.ap, i=psc[:, 0:1024].ap, b=ngm.ap:
              e.activation(out=o, in_=i, func=AF.Exp, bias=b), [psc[:, 0:1024], ngm], [gpb])
            gs = P.fstat[:, 42:43]
            E("V", lambda e, o=gs.ap, i=gpb.ap: e.tensor_reduce(out=o, in_=i, axis=AX.X, op=ALU.add),
              [gpb], [gs])
            for (srccol, dstcol) in ((40, grp * 4 + half * 2), (42, grp * 4 + half * 2 + 1)):
                s_ = P.fstat[:, srccol:srccol + 1]
                d_ = cv["gst"][:, dstcol:dstcol + 1]
                E("V", lambda e, o=d_.ap, i=s_.ap: e.tensor_copy(out=o, in_=i), [s_], [d_])
            for kc in range(8):
                ptp = P.bank(5, BF16)[0:128, (kc % 4) * 128:((kc % 4) + 1) * 128]
                s_ = gpv[:, kc * 128:(kc + 1) * 128]
                E("T", lambda e, o=ptp.ap, i=s_.ap, idm=P.idbf[:, :].ap:
                  e.transpose(out=o, in_=i, identity=idm), [s_, P.idbf[:, :]], [ptp])
                d_ = pT[:, kc * 128:(kc + 1) * 128]
                E("V", lambda e, o=d_.ap, i=ptp.ap: e.tensor_copy(out=o, in_=i), [ptp], [d_])
            for j, h in enumerate(heads):
                po = P.bank(3)[0:64, h * 8:(h + 1) * 8]
                for kc in range(8):
                    vsl = V_[:, kc * 1024 + h * 64: kc * 1024 + (h + 1) * 64]
                    dk = pT[:, kc * 128:(kc + 1) * 128]
                    rk = APK(dk.ap[:, 32 * j:32 * j + 8], dk.keys)
                    E("T", lambda e, o=po.ap, l=vsl.ap, r=rk.ap, st=(kc == 0), sp=(kc == 7):
                      e.matmul(out=o, lhsT=l, rhs=r, start=st, stop=sp), [vsl, rk], [po])
            for h in heads:
                po = P.bank(3)[0:64, h * 8:(h + 1) * 8]
                d_ = og[0:64, h * 8:(h + 1) * 8]
                E("V", lambda e, o=d_.ap, i=po.ap: e.tensor_copy(out=o, in_=i), [po], [d_])

    gst = cv["gst"]
    def vv(o, i, j, op):
        E("V", lambda e, o_=o.ap, i_=i.ap, j_=j.ap: e.tensor_tensor(out=o_, in0=i_, in1=j_, op=op),
          [i, j], [o])
    for grp in (range(6) if "comb" in ATTN0_PARTS else []):
        heads = [3 * grp + j for j in range(3) if 3 * grp + j < NH]
        g4 = grp * 4
        m0, s0, m1, s1 = gst[:, g4:g4 + 1], gst[:, g4 + 1:g4 + 2], gst[:, g4 + 2:g4 + 3], gst[:, g4 + 3:g4 + 4]
        M, nM = gst[:, 24:25], gst[:, 25:26]
        f0, f1 = gst[:, 26:27], gst[:, 27:28]
        den, t2_, rc = gst[:, 28:29], gst[:, 29:30], gst[:, 30:31]
        c0, c1 = gst[:, 31:32], gst[:, 32:33]
        vv(M, m0, m1, ALU.max)
        E("V", lambda e, o=nM.ap, i=M.ap: e.tensor_single_scalar(out=o, in_=i, scalar=-1.0, op=ALU.mult),
          [M], [nM])
        E("S", lambda e, o=f0.ap, i=m0.ap, b=nM.ap: e.activation(out=o, in_=i, func=AF.Exp, bias=b),
          [m0, nM], [f0])
        E("S", lambda e, o=f1.ap, i=m1.ap, b=nM.ap: e.activation(out=o, in_=i, func=AF.Exp, bias=b),
          [m1, nM], [f1])
        vv(den, s0, f0, ALU.mult)
        vv(t2_, s1, f1, ALU.mult)
        vv(den, den, t2_, ALU.add)
        E("V", lambda e, o=rc.ap, i=den.ap: e.reciprocal(out=o, in_=i), [den], [rc])
        vv(c0, f0, rc, ALU.mult)
        vv(c1, f1, rc, ALU.mult)
        for (csrc, coff) in ((c0, 0), (c1, 128)):
            cbf = P.tmp1[:, 0:1]
            E("V", lambda e, o=cbf.ap, i=csrc.ap: e.tensor_copy(out=o, in_=i), [csrc], [cbf])
            ptp = P.bank(5, BF16)[0:1, 0:128]
            E("T", lambda e, o=ptp.ap, i=cbf.ap, idm=P.idbf[:, :].ap:
              e.transpose(out=o, in_=i, identity=idm), [cbf, P.idbf[:, :]], [ptp])
            rowt = P.tmp2[0:1, 0:128]
            E("V", lambda e, o=rowt.ap, i=ptp.ap: e.tensor_copy(out=o, in_=i), [ptp], [rowt])
            for j, h in enumerate(heads):
                s_ = P.tmp2[0:1, 32 * j:32 * j + 8]
                d_ = cv["crow"][0:1, coff + h * 8:coff + (h + 1) * 8]
                E("V", lambda e, o=d_.ap, i=s_.ap: e.tensor_copy(out=o, in_=i), [s_], [d_])
    for (coff, cdst) in (((0, cv["cb0"]), (128, cv["cb1"])) if "comb" in ATTN0_PARTS else ()):
        crow = cv["crow"][0:1, coff:coff + 128]
        pb = P.bank(3)[0:64, 128:256]
        E("T", lambda e, o=pb.ap, l=P.ones_r[0:1, 0:64].ap, r=crow.ap:
          e.matmul(out=o, lhsT=l, rhs=r, start=True, stop=True), [crow, P.ones_r[0:1, 0:64]], [pb])
        E("V", lambda e, o=cdst[0:64, 0:128].ap, i=pb.ap: e.tensor_copy(out=o, in_=i), [pb], [cdst[0:64, 0:128]])
    ogf = cv["ogf"]
    if "comb" not in ATTN0_PARTS:
        return
    E("V", lambda e, o=ogf[0:64, 0:128].ap, i=cv["og0"][0:64, 0:128].ap, j=cv["cb0"][0:64, 0:128].ap:
      e.tensor_tensor(out=o, in0=i, in1=j, op=ALU.mult),
      [cv["og0"][0:64, 0:128], cv["cb0"][0:64, 0:128]], [ogf[0:64, 0:128]])
    E("V", lambda e, o=cv["og0"][0:64, 0:128].ap, i=cv["og1"][0:64, 0:128].ap, j=cv["cb1"][0:64, 0:128].ap:
      e.tensor_tensor(out=o, in0=i, in1=j, op=ALU.mult),
      [cv["og1"][0:64, 0:128], cv["cb1"][0:64, 0:128]], [cv["og0"][0:64, 0:128]])
    E("V", lambda e, o=ogf[0:64, 0:128].ap, i=ogf[0:64, 0:128].ap, j=cv["og0"][0:64, 0:128].ap:
      e.tensor_tensor(out=o, in0=i, in1=j, op=ALU.add),
      [ogf[0:64, 0:128], cv["og0"][0:64, 0:128]], [ogf[0:64, 0:128]])
    for h in (range(NH) if "comb" in ATTN0_PARTS else []):
        prow, chh = (h % 2) * 64, h // 2
        for gi, tg in enumerate(GLOBAL_TOKENS):
            s_ = ogf[0:64, h * 8 + gi:h * 8 + gi + 1]
            d_ = P.xh[prow:prow + 64, chh * S + tg: chh * S + tg + 1]
            E("V", lambda e, o=d_.ap, i=s_.ap: e.tensor_copy(out=o, in_=i), [s_], [d_])

    if "outproj" in ATTN0_PARTS:
        _outproj(P, cv, "L0_attn_Wo", "L0_attn_bo")


def emit_attn1(P):
    E = P.E
    cv = _attn_carves(P)
    K_, Q_, V_, W_ = cv["K"], cv["Q"], cv["V"], cv["W"]
    KVs, am, ktm, rr, rb = cv["KVs"], cv["am"], cv["ktm"], cv["rr"], cv["rb"]
    KVb = cv["KVb"]
    P.dma("G", P.rowA[0:1, 0:1024], APK(P.din("L1_attn_bq", [DM]).unsqueeze(0), []))
    P.dma("G", P.rowB[0:1, 0:1024], APK(P.din("L1_attn_bk", [DM]).unsqueeze(0), []))
    P.dma("G", P.rowC[0:1, 0:1024], APK(P.din("L1_attn_bv", [DM]).unsqueeze(0), []))
    E("V", lambda e, o=KVs[0:128, 0:16 * 65].ap: e.memset(o, 0.0), [], [KVs[0:128, 0:16 * 65]])
    E("V", lambda e, o=KVb[0:128, 0:16 * 65].ap: e.memset(o, 0.0), [], [KVb[0:128, 0:16 * 65]])

    for half in range(2):
        t0, t1 = half * HS, (half + 1) * HS
        _load_wmat(P, cv, "L1_attn_Wq")
        _proj_fm(P, W_, Q_, P.rowA, t0, t1, phi=True, bcol=P.bcA, build_cols=(half == 0))
        _load_wmat(P, cv, "L1_attn_Wk")
        _proj_fm(P, W_, K_, P.rowB, t0, t1, phi=True, bcol=P.bcB, build_cols=(half == 0))
        _load_wmat(P, cv, "L1_attn_Wv")
        _proj_tm(P, W_, V_, P.rowC, t0, t1)

        for cl in range(8):
            cg = half * 8 + cl
            for hg in range(4):
                hs = []
                for j in range(4):
                    h = hg * 4 + j
                    prow, chh = (h % 2) * 64, h // 2
                    kf = K_[prow:prow + 64, chh * 1024 + cl * 128: chh * 1024 + (cl + 1) * 128]
                    qf = Q_[prow:prow + 64, chh * 1024 + cl * 128: chh * 1024 + (cl + 1) * 128]
                    vt = V_[:, cl * 1024 + h * 64: cl * 1024 + (h + 1) * 64]
                    ams = am[:, j * 128:(j + 1) * 128]
                    kts = ktm[:, j * 64:(j + 1) * 64]
                    hs.append((h, prow, kf, qf, vt, ams, kts))
                for (h, prow, kf, qf, vt, ams, kts) in hs:
                    j = h % 4
                    pa = P.bank(0)[:, j * 128:(j + 1) * 128]
                    E("T", lambda e, o=pa.ap, l=kf.ap, r=qf.ap:
                      e.matmul(out=o, lhsT=l, rhs=r, start=True, stop=True), [kf, qf], [pa])
                    pt = P.bank(5, BF16)[0:128, j * 64:(j + 1) * 64]
                    idsl = P.idbf[prow:prow + 64, prow:prow + 64]
                    E("T", lambda e, o=pt.ap, i=kf.ap, idm=idsl.ap:
                      e.transpose(out=o, in_=i, identity=idm), [kf, idsl], [pt])
                    E("V", lambda e, o=ams.ap, i=pa.ap, m=P.tri[:, :].ap:
                      e.tensor_tensor(out=o, in0=i, in1=m, op=ALU.mult), [pa, P.tri[:, :]], [ams])
                    E("V", lambda e, o=kts.ap, i=pt.ap: e.tensor_copy(out=o, in_=i), [pt], [kts])
                for (h, prow, kf, qf, vt, ams, kts) in hs:
                    j = h % 4
                    pnum = P.bank(1)[0:64, j * 128:(j + 1) * 128]
                    pden = P.bank(4)[0:1, j * 128:(j + 1) * 128]
                    kv64 = KVb[prow:prow + 64, h * 65: h * 65 + 64]
                    ks1 = KVb[prow:prow + 64, h * 65 + 64: h * 65 + 65]
                    E("T", lambda e, o=pnum.ap, l=kv64.ap, r=qf.ap:
                      e.matmul(out=o, lhsT=l, rhs=r, start=True, stop=False), [kv64, qf], [pnum])
                    E("T", lambda e, o=pnum.ap, l=vt.ap, r=ams.ap:
                      e.matmul(out=o, lhsT=l, rhs=r, start=False, stop=True), [vt, ams], [pnum])
                    E("T", lambda e, o=pden.ap, l=ks1.ap, r=qf.ap:
                      e.matmul(out=o, lhsT=l, rhs=r, start=True, stop=False), [ks1, qf], [pden])
                    E("T", lambda e, o=pden.ap, l=P.ones_c[:, 0:1].ap, r=ams.ap:
                      e.matmul(out=o, lhsT=l, rhs=r, start=False, stop=True),
                      [P.ones_c[:, 0:1], ams], [pden])
                    sl = ((h // 2) % 4) * 66
                    kvp = P.bank(6)[prow:prow + 64, sl:sl + 64]
                    E("T", lambda e, o=kvp.ap, l=kts.ap, r=vt.ap:
                      e.matmul(out=o, lhsT=l, rhs=r, start=True, stop=True), [kts, vt], [kvp])
                    ksp = P.bank(6)[prow:prow + 64, sl + 64:sl + 65]
                    E("T", lambda e, o=ksp.ap, l=kts.ap, r=P.ones_c[:, 0:1].ap:
                      e.matmul(out=o, lhsT=l, rhs=r, start=True, stop=True),
                      [kts, P.ones_c[:, 0:1]], [ksp])
                    dr = rr[0:1, j * 128:(j + 1) * 128]
                    E("V", lambda e, o=dr.ap, i=pden.ap:
                      e.tensor_single_scalar(out=o, in_=i, scalar=1e-6, op=ALU.add), [pden], [dr])
                    E("V", lambda e, o=dr.ap, i=dr.ap: e.reciprocal(out=o, in_=i), [dr], [dr])
                    kvf = KVs[prow:prow + 64, h * 65: h * 65 + 65]
                    kvpf = P.bank(6)[prow:prow + 64, sl:sl + 65]
                    E("V", lambda e, o=kvf.ap, i=kvpf.ap, x=kvf.ap:
                      e.tensor_tensor(out=o, in0=i, in1=x, op=ALU.add), [kvpf, kvf], [kvf])
                    kvb = KVb[prow:prow + 64, h * 65: h * 65 + 65]
                    E("V", lambda e, o=kvb.ap, i=kvf.ap: e.tensor_copy(out=o, in_=i), [kvf], [kvb])
                rrs = rr[0:1, 0:512]
                pb = P.bank(3)[0:64, 0:512]
                E("T", lambda e, o=pb.ap, l=P.ones_f[0:1, 0:64].ap, r=rrs.ap:
                  e.matmul(out=o, lhsT=l, rhs=r, start=True, stop=True), [rrs, P.ones_f[0:1, 0:64]], [pb])
                E("V", lambda e, o=rb[0:64, 0:512].ap, i=pb.ap: e.tensor_copy(out=o, in_=i),
                  [pb], [rb[0:64, 0:512]])
                for j in range(4):
                    h = hg * 4 + j
                    prow, chh = (h % 2) * 64, h // 2
                    pnum = P.bank(1)[0:64, j * 128:(j + 1) * 128]
                    rbs = rb[0:64, j * 128:(j + 1) * 128]
                    ao = P.xh[prow:prow + 64,
                              chh * S + half * HS + cl * 128: chh * S + half * HS + (cl + 1) * 128]
                    E("V", lambda e, o=ao.ap, i=pnum.ap, j2=rbs.ap:
                      e.tensor_tensor(out=o, in0=i, in1=j2, op=ALU.mult), [pnum, rbs], [ao])

    _outproj(P, cv, "L1_attn_Wo", "L1_attn_bo")

